# version 19
# baseline (speedup 1.0000x reference)
import numpy as np
import ml_dtypes
import concourse.bass as bass
import concourse.mybir as mybir
from concourse.bass_utils import run_bass_kernel_spmd
import concourse.dve_ops as dve_ops_mod
from concourse.dve_op import DveOp
from concourse.dve_spec import Spec, Src0, Src1, C0, C1, C2, Zero, One, lower, maxx, minn

F32 = mybir.dt.float32
BF16 = mybir.dt.bfloat16
ALU = mybir.AluOpType
ACT = mybir.ActivationFunctionType
AXX = mybir.AxisListType.X
bf16 = ml_dtypes.bfloat16

MAGIC = float(1.5 * 2.0 ** 23)
S, DM, DQKV, DFF = 2048, 1024, 3072, 4096
NSUB = S // 128
ND = DM // 128
NPAIR = 8
NM = 11
NCF = DFF // 128
SCH = 2
NSUP = NSUB // SCH
EPS = 1e-5


def _register_ops():
    ver = "v3"

    def mk(name, body, ref):
        low = lower(Spec(body=body), ver=ver, perf_en=False, name=name)
        low.validate(ver)
        return DveOp(name, Spec(body=body), subdim=False,
                     uops_sha={ver: low.sha(ver)}, reference=ref)

    ra = mk("ANT_ROUND_AFFINE", ((Src0 * C0 + C1) + C2) - C2,
            lambda in0, in1, s0, s1, imm2: np.round(in0 * s0 + s1))
    rm = mk("ANT_ROUND_MUL", ((Src0 * Src1 + C0) + C2) - C2,
            lambda in0, in1, s0, s1, imm2: np.round(in0 * in1 + s0))
    wq = mk("ANT_WQUANT", minn(maxx(((Src0 * C0 + C2) - C2), Zero - One), One) * C1,
            lambda in0, in1, s0, s1, imm2:
            np.minimum(np.maximum(np.round(in0 * s0), -1.0), 1.0) * s1)
    cl = mk("ANT_CLIP", minn(maxx(Src0 * C0, Zero - One), One),
            lambda in0, in1, s0, s1, imm2:
            np.minimum(np.maximum(in0 * s0, -1.0), 1.0))
    mine = [ra, rm, wq, cl]
    have = {o.name for o in dve_ops_mod.OPS}
    newops = [o for o in dve_ops_mod.OPS if not o.name.startswith("DVEHASH")]
    newops += [o for o in mine if o.name not in have]
    dve_ops_mod.OPS[:] = newops
    dve_ops_mod._SUB_OPCODE_FOR_NAME.clear()
    dve_ops_mod._SUB_OPCODE_FOR_NAME.update(
        {o.name: dve_ops_mod._CUSTOM_DVE_ROW_BASE + i for i, o in enumerate(newops)})
    dve_ops_mod.CUSTOM_DVE_SPECS.update({o.name: o.spec for o in mine})
    dve_ops_mod.CUSTOM_DVE_REFS.update({o.name: o.ref for o in mine})
    return ra, rm, wq, cl


ROUND_AFFINE, ROUND_MUL, WQUANT, CLIP = _register_ops()


class Sem:
    def __init__(self, es, nc, name):
        self.h = es.enter_context(nc.semaphore(name))
        self.n = 0

    def inc(self, instr, k=1):
        instr.then_inc(self.h, k)
        self.n += k
        return instr


class Stream:

    def __init__(self, es, nc, name, ns=2):
        self.ns = ns
        self.sems = [Sem(es, nc, f"{name}_{k}") for k in range(ns)]
        self.i = 0

    def xfer(self, eng, dst, src, **kw):
        s = self.sems[self.i % self.ns]
        s.inc(eng.dma_start(dst, src, **kw), 16)
        self.i += 1

    def spec(self, idx):
        return (self.sems[idx % self.ns], 16 * (idx // self.ns + 1))

    def wait_idx(self, eng, idx):
        sem, cnt = self.spec(idx)
        eng.wait_ge(sem.h, cnt)


class XQ:

    def __init__(self, vsem, ssem):
        self.vsem, self.ssem = vsem, ssem
        self.vbase, self.sbase = vsem.n, ssem.n
        self.ops = []
        self.marks = {}

    def v(self, fn, drain=False, waits=()):
        self.ops.append(("v", fn, drain, list(waits)))

    def s(self, fn, drain=False, waits=()):
        self.ops.append(("s", fn, drain, list(waits)))

    def mark(self, name):
        nv = sum(1 for o in self.ops if o[0] == "v")
        ns = sum(1 for o in self.ops if o[0] == "s")
        self.marks[name] = (self.vbase + nv, self.sbase + ns)

    def wait_mark(self, eng, name):
        v_abs, s_abs = self.marks[name]
        if v_abs > 0:
            eng.wait_ge(self.vsem.h, v_abs)
        if s_abs > 0:
            eng.wait_ge(self.ssem.h, s_abs)

    def run(self, which, eng):
        nv, ns = self.vbase, self.sbase
        last_cross = -1
        last_waits = {}
        mysem = self.vsem if which == "v" else self.ssem
        othersem = self.ssem if which == "v" else self.vsem
        for e, fn, drain, waits in self.ops:
            if e != which:
                if e == "v":
                    nv += 1
                else:
                    ns += 1
                continue
            cross = ns if which == "v" else nv
            if cross > last_cross and cross > 0:
                eng.wait_ge(othersem.h, cross)
                last_cross = cross
            for sem, cnt in waits:
                if cnt > 0 and last_waits.get(id(sem), -1) < cnt:
                    eng.wait_ge(sem.h, cnt)
                    last_waits[id(sem)] = cnt
            if drain:
                eng.drain()
            instr = fn(eng)
            assert instr is not None, "XQ op fn must return an instruction"
            mysem.inc(instr)
            if which == "v":
                nv += 1
            else:
                ns += 1
        if which == "v":
            assert self.vsem.n == nv, (self.vsem.n, nv)
        else:
            assert self.ssem.n == ns, (self.ssem.n, ns)


def emit_cheb(xq, src, sc, out_of, tw=None, on_feat=None):
    sqa, sqb, ub = sc["sqa"], sc["sqb"], sc["ub"]
    pa, pb, pc = sc["pa"], sc["pb"], sc["pc"]
    w = tw if tw else (lambda m: ())
    nf = on_feat if on_feat else (lambda m: None)
    o2, o3, o4, o5 = out_of(2), out_of(3), out_of(4), out_of(5)
    xq.s(lambda e: e.copy(out_of(1), src), waits=w(1)); nf(1)
    xq.v(lambda e: e.tensor_tensor(sqa, src, src, ALU.mult), drain=True)
    xq.v(lambda e: e.tensor_scalar(o2, sqa, 2.0, -1.0, ALU.mult, ALU.add),
         drain=True, waits=w(2)); nf(2)
    xq.v(lambda e: e.tensor_scalar(ub, o2, 2.0, -1.0, ALU.mult, ALU.add),
         drain=True)
    xq.v(lambda e: e.tensor_tensor(o3, src, ub, ALU.mult), drain=True,
         waits=w(3)); nf(3)
    xq.v(lambda e: e.tensor_tensor(sqb, o2, o2, ALU.mult))
    xq.v(lambda e: e.tensor_scalar(o4, sqb, 2.0, -1.0, ALU.mult, ALU.add),
         drain=True, waits=w(4)); nf(4)
    xq.v(lambda e: e.tensor_tensor(pa, o3, o2, ALU.mult))
    xq.v(lambda e: e.scalar_tensor_tensor(o5, pa, 2.0, src, ALU.mult,
                                          ALU.subtract), drain=True,
         waits=w(5)); nf(5)
    xq.s(lambda e: e.activation(sqa, o3, ACT.Square))
    xq.v(lambda e: e.tensor_tensor(pb, o4, o3, ALU.mult))
    xq.v(lambda e: e.tensor_scalar(out_of(6), sqa, 2.0, -1.0, ALU.mult, ALU.add),
         waits=w(6)); nf(6)
    xq.s(lambda e: e.activation(ub, o4, ACT.Square))
    xq.v(lambda e: e.tensor_tensor(pc, o5, o4, ALU.mult))
    xq.v(lambda e: e.scalar_tensor_tensor(out_of(7), pb, 2.0, src, ALU.mult,
                                          ALU.subtract), drain=True,
         waits=w(7)); nf(7)
    xq.s(lambda e: e.activation(sqb, o5, ACT.Square))
    xq.v(lambda e: e.tensor_scalar(out_of(8), ub, 2.0, -1.0, ALU.mult, ALU.add),
         waits=w(8)); nf(8)
    xq.v(lambda e: e.scalar_tensor_tensor(out_of(9), pc, 2.0, src, ALU.mult,
                                          ALU.subtract), drain=True,
         waits=w(9)); nf(9)
    xq.v(lambda e: e.tensor_scalar(out_of(10), sqb, 2.0, -1.0, ALU.mult, ALU.add),
         waits=w(10)); nf(10)


class XQRec:

    def __init__(self):
        self.items = []

    def v(self, fn, drain=False, waits=()):
        self.items.append(("v", fn, drain, list(waits)))

    def s(self, fn, drain=False, waits=()):
        self.items.append(("s", fn, drain, list(waits)))

    def mark(self, name):
        self.items.append(("mark", name, None, None))


def xq_merge(xq, rec_a, rec_b, ratio=2):
    def emit(item):
        kind, x, drain, waits = item
        if kind == "mark":
            xq.mark(x)
        elif kind == "v":
            xq.v(x, drain=drain, waits=waits)
        else:
            xq.s(x, drain=drain, waits=waits)

    bi = 0
    bitems = rec_b.items if rec_b else []
    inserted = [0]

    def pop_b():
        nonlocal bi
        if bi >= len(bitems):
            return
        emit(bitems[bi])
        if bitems[bi][0] == "v":
            inserted[0] += 1
        bi += 1
        while bi < len(bitems) and bitems[bi][0] == "mark":
            emit(bitems[bi])
            bi += 1

    nv = 0
    for item in rec_a.items:
        emit(item)
        if item[0] == "v":
            inserted[0] = 0
            nv += 1
            if nv % ratio == 0:
                pop_b()
                pop_b()
    while bi < len(bitems):
        emit(bitems[bi])
        bi += 1


def dma_nop(eng, dma_sem, dst, src, **kw):
    dma_sem.inc(eng.dma_start(dst, src, **kw), 16)
    return eng.nop()


def build(dbg=(), upto=99):
    import os
    from contextlib import ExitStack
    upto = int(os.environ.get("ANT_UPTO", upto))
    dbg = set(dbg)
    nc = bass.Bass("TRN2", target_bir_lowering=False,
                   dynamic_dma_scratch_size=16384)

    x_hd = nc.dram_tensor("x", [S, DM], F32, kind="ExternalInput")
    w_inT_hd = nc.dram_tensor("w_inT", [DM, DQKV], F32, kind="ExternalInput")
    w_outT_hd = nc.dram_tensor("w_outT", [DM, DM], F32, kind="ExternalInput")
    w1T_hd = nc.dram_tensor("w1T", [DFF, DM], F32, kind="ExternalInput")
    w2T_hd = nc.dram_tensor("w2T", [DFF, DM], F32, kind="ExternalInput")
    convw_in = nc.dram_tensor("convw", [128, NCF * 3], F32, kind="ExternalInput").ap()
    convb_in = nc.dram_tensor("convb", [128, NCF], F32, kind="ExternalInput").ap()
    beta_in = nc.dram_tensor("beta_t", [128, 176], F32, kind="ExternalInput").ap()
    identb_in = nc.dram_tensor("identb", [128, 128], BF16, kind="ExternalInput").ap()
    identf_in = nc.dram_tensor("identf", [128, 128], F32, kind="ExternalInput").ap()
    onesf_in = nc.dram_tensor("onesf", [128, 128], F32, kind="ExternalInput").ap()
    sel_in = nc.dram_tensor("sel", [16, NSUB * 128], F32, kind="ExternalInput").ap()
    x_in = x_hd.ap()
    w_inT = w_inT_hd.ap()
    w_outT_h = w_outT_hd.ap()
    w1T_h = w1T_hd.ap()
    w2T_h = w2T_hd.ap()
    out_hb = nc.dram_tensor("out", [S, DM], F32, kind="ExternalOutput").ap()
    xn_h = nc.dram_tensor("xn_h", [S, DM], BF16).ap()
    xq_h = nc.dram_tensor("xq_h", [S, DM], BF16).ap()
    x1_h = nc.dram_tensor("x1_h", [S, DM], F32).ap()
    attn_h = nc.dram_tensor("attn_h", [DM, S], BF16).ap()
    hq_hd = nc.dram_tensor("hq_h", [DFF, S], BF16)
    hq_h = hq_hd.ap()

    dbg_aps = {}
    for name, shape, dt in [("xn", [S, DM], BF16), ("kvt", [128, NPAIR * 704], F32),
                            ("kvbd", [128, NPAIR * 1408], BF16),
                            ("attn", [DM, S], BF16),
                            ("x1", [S, DM], F32), ("xq", [S, DM], BF16),
                            ("hq", [DFF, S], BF16), ("wm", [128, 8], F32),
                            ("stile", [128, S], F32), ("hs", [128, NCF * S], BF16),
                            ("rmx", [128, 2 * S], BF16),
                            ("xqt", [128, ND * S], BF16),
                            ("w1qd", [128, NCF * DM], BF16)]:
        if name in dbg:
            dbg_aps[name] = nc.dram_tensor("dbg_" + name, shape, dt,
                                           kind="ExternalOutput").ap()

    es = ExitStack()
    with es:
        identb = es.enter_context(nc.sbuf_tensor("identb_s", [128, 128], BF16))
        identf = es.enter_context(nc.sbuf_tensor("identf_s", [128, 128], F32))
        onesf = es.enter_context(nc.sbuf_tensor("onesf_s", [128, 128], F32))
        beta_s = es.enter_context(nc.sbuf_tensor("beta_s", [128, 176], F32))
        convw = es.enter_context(nc.sbuf_tensor("convw_s", [128, NCF * 3], F32))
        convb = es.enter_context(nc.sbuf_tensor("convb_s", [128, NCF], F32))
        st2 = es.enter_context(nc.sbuf_tensor("st2", [128, 160], F32))
        st3 = es.enter_context(nc.sbuf_tensor("st3", [128, 64], F32))
        wmean = es.enter_context(nc.sbuf_tensor("wmean", [128, 8], F32))
        wm1 = es.enter_context(nc.sbuf_tensor("wm1", [1, 4], F32))

        dsem = Sem(es, nc, "dsem")
        gsem = Sem(es, nc, "gsem")
        xsem = Sem(es, nc, "xsem")
        tsem = Sem(es, nc, "tsem")
        vsem = Sem(es, nc, "vsem")
        ssem = Sem(es, nc, "ssem")
        osem = Sem(es, nc, "osem")
        xisem = Sem(es, nc, "xisem")
        st_wabs = Stream(es, nc, "stwab", ns=8)
        st_xn = Stream(es, nc, "stxn")
        st_ast = Stream(es, nc, "stast")
        st_xq = Stream(es, nc, "stxq")
        st_w1 = Stream(es, nc, "stw1")
        st_w2 = Stream(es, nc, "stw2", ns=16)
        st_hq = Stream(es, nc, "sthq", ns=4)
        st_x1t = Stream(es, nc, "stx1t")
        st_out = Stream(es, nc, "stout")
        streams = [st_wabs, st_xn, st_ast, st_xq, st_w1, st_w2, st_hq,
                   st_x1t, st_out]
        allsems = ([dsem, gsem, xsem, tsem, vsem, ssem, osem, xisem]
                   + [s for st in streams for s in st.sems])

        _fence_counts = [{}]

        def snap_fence():
            _fence_counts[0] = {id(sem): sem.n for sem in allsems}

        def fence(eng):
            for sem in allsems:
                cnt = _fence_counts[0].get(id(sem), 0)
                if cnt > 0:
                    eng.wait_ge(sem.h, cnt)

        with ExitStack() as _es1:
            xall = _es1.enter_context(nc.sbuf_tensor("xall", [128, NSUB * DM], F32))
            wabs = _es1.enter_context(nc.sbuf_tensor("wabs", [128, 8 * DM], F32))
            sqdump = _es1.enter_context(nc.sbuf_tensor("sqdump", [128, DFF], F32))
            wpart = _es1.enter_context(nc.sbuf_tensor("wpart", [128, 64], F32))
            xnt = _es1.enter_context(nc.sbuf_tensor("xnt", [128, 2 * DM], BF16))
            ps_m = _es1.enter_context(nc.psum_tensor("ps_m", [128, 1536], F32))
            blk = _es1.enter_context(nc.Block())
            snap_fence()
            d0 = dsem.n
            dmk_all = d0 + 16 * 6 + 16 * NSUB
            t0 = tsem.n

            xq0 = XQ(vsem, ssem)
            xq0.v(lambda e: e.memset(wmean[:], 0.0))
            xq0.v(lambda e: e.memset(wmean[:, 7:8], EPS), drain=True)
            ci = 0
            for j in range(2):
                for i in range(DFF // 128):
                    src = wabs[:, (ci % 8) * DM:(ci % 8) * DM + DM]
                    col = wpart[:, j * 32 + i:j * 32 + i + 1]
                    xq0.s((lambda s_, c_: lambda e: e.activation(
                        sqdump[:, 0:DM], s_, ACT.Abs, accum_out=c_))(src, col),
                        waits=[st_wabs.spec(ci)], drain=True)
                    xq0.mark(f"wa{ci}")
                    ci += 1
            xq0.mark("wabs")
            for i in range(NSUB):
                xq0.v((lambda i_: lambda e: e.reduce_sum(
                    st2[:, i_:i_ + 1], xall[:, i_ * DM:(i_ + 1) * DM], axis=AXX))(i),
                    waits=[(dsem, dmk_all)])
            for i in range(NSUB):
                xq0.s((lambda i_: lambda e: e.activation(
                    sqdump[:, 0:DM], xall[:, i_ * DM:(i_ + 1) * DM], ACT.Square,
                    accum_out=st2[:, 16 + i_:17 + i_]))(i),
                    waits=[(dsem, dmk_all)], drain=True)
            xq0.v(lambda e: e.tensor_scalar(st2[:, 32:48], st2[:, 0:16],
                                            1.0 / DM, None, ALU.mult), drain=True)
            xq0.v(lambda e: e.tensor_tensor(st2[:, 48:64], st2[:, 32:48],
                                            st2[:, 32:48], ALU.mult), drain=True)
            xq0.v(lambda e: e.scalar_tensor_tensor(st2[:, 48:64], st2[:, 16:32],
                                                   1.0 / DM, st2[:, 48:64],
                                                   ALU.mult, ALU.subtract), drain=True)
            xq0.s(lambda e: e.activation(st2[:, 48:64], st2[:, 48:64], ACT.Sqrt,
                                         bias=wmean[:, 7:8]), drain=True)
            xq0.v(lambda e: e.reciprocal(st2[:, 64:80], st2[:, 48:64]), drain=True)
            xq0.v(lambda e: e.reduce_sum(st3[:, 0:1], wpart[:, 0:32], axis=AXX),
                  drain=True)
            xq0.v(lambda e: e.reduce_sum(st3[:, 1:2], wpart[:, 32:64], axis=AXX))
            xq0.mark("wsums")
            xn0 = st_xn.i
            for i in range(NSUB):
                xq0.v((lambda i_: lambda e: e.tensor_scalar(
                    xnt[:, (i_ % 2) * DM:(i_ % 2) * DM + DM],
                    xall[:, i_ * DM:(i_ + 1) * DM],
                    st2[:, 32 + i_:33 + i_], st2[:, 64 + i_:65 + i_],
                    ALU.subtract, ALU.mult))(i), drain=True,
                    waits=([st_xn.spec(xn0 + i - 2)] if i >= 2 else []))
                xq0.s((lambda i_: lambda e: (st_xn.xfer(
                    e, xn_h[i_ * 128:(i_ + 1) * 128, :],
                    xnt[:, (i_ % 2) * DM:(i_ % 2) * DM + DM]), e.nop())[1])(i))
            xq0.mark("xnst")
            xq0.v(lambda e: e.tensor_scalar(wm1[0:1, 0:1], ps_m[0:1, 0:1],
                                            1.0 / (DM * DFF), EPS, ALU.mult,
                                            ALU.max), waits=[(tsem, t0 + 1)])
            xq0.v(lambda e: e.tensor_scalar(wm1[0:1, 1:2], ps_m[0:1, 512:513],
                                            1.0 / (DM * DFF), EPS, ALU.mult,
                                            ALU.max))
            xq0.mark("wm1")
            xq0.v(lambda e: e.tensor_copy(wmean[:, 4:6], ps_m[:, 1024:1026]),
                  waits=[(tsem, t0 + 2)], drain=True)
            xq0.v(lambda e: e.tensor_copy(wmean[:, 1:2], wmean[:, 4:5]), drain=True)
            xq0.v(lambda e: e.tensor_copy(wmean[:, 3:4], wmean[:, 5:6]))
            xq0.v(lambda e: e.reciprocal(wmean[:, 0:1], wmean[:, 4:5]), drain=True)
            xq0.v(lambda e: e.reciprocal(wmean[:, 2:3], wmean[:, 5:6]))
            xq0.mark("wmean")
            xq0.v(lambda e: e.tensor_scalar(convw[:], convw[:], wmean[:, 1:2],
                                            None, ALU.mult),
                  drain=True, waits=[(dsem, dmk_all)])
            xq0.mark("convfold")

            @blk.sync
            def _(sync):
                for dst, src in [(identb[:], identb_in), (identf[:], identf_in),
                                 (onesf[:], onesf_in), (beta_s[:], beta_in),
                                 (convw[:], convw_in), (convb[:], convb_in)]:
                    dsem.inc(sync.dma_start(dst, src), 16)
                for i in range(NSUB):
                    dsem.inc(sync.dma_start(xall[:, i * DM:(i + 1) * DM],
                                            x_in[i * 128:(i + 1) * 128, :]), 16)
                assert dsem.n == dmk_all

            @blk.gpsimd
            def _(gpsimd):
                ci = 0
                for j in range(2):
                    wsrc = w1T_h if j == 0 else w2T_h
                    for i in range(DFF // 128):
                        if ci >= 8:
                            xq0.wait_mark(gpsimd, f"wa{ci - 8}")
                        st_wabs.xfer(gpsimd,
                                     wabs[:, (ci % 8) * DM:(ci % 8) * DM + DM],
                                     wsrc[i * 128:(i + 1) * 128, :])
                        ci += 1

            @blk.vector
            def _(vector):
                xq0.run("v", vector)

            @blk.scalar
            def _(scalar):
                xq0.run("s", scalar)

            @blk.tensor
            def _(tensor):
                xq0.wait_mark(tensor, "wsums")
                tensor.matmul(ps_m[0:1, 0:1], onesf[:, 0:1], st3[:, 0:1],
                              start=True, stop=True)
                tsem.inc(tensor.matmul(ps_m[0:1, 512:513], onesf[:, 0:1],
                                       st3[:, 1:2], start=True, stop=True))
                xq0.wait_mark(tensor, "wm1")
                tsem.inc(tensor.matmul(ps_m[:, 1024:1026], onesf[0:1, 0:128],
                                       wm1[0:1, 0:2], start=True, stop=True))
                assert tsem.n == t0 + 2

            if "wm" in dbg_aps or "xn" in dbg_aps:
                @blk.sync
                def _(sync):
                    if "wm" in dbg_aps:
                        xq0.wait_mark(sync, "wmean")
                        osem.inc(sync.dma_start(dbg_aps["wm"], wmean[:]), 16)
                    if "xn" in dbg_aps:
                        for sm in st_xn.sems:
                            sync.wait_ge(sm.h, sm.n)
                        osem.inc(sync.dma_start(dbg_aps["xn"], xn_h), 16)

        if upto >= 1:
            with ExitStack() as _es2:
                kv_bd = _es2.enter_context(nc.sbuf_tensor("kv_bd", [128, NPAIR * 1408], BF16))
                xnT = _es2.enter_context(nc.sbuf_tensor("xnT", [128, ND * S], BF16))
                with ExitStack() as _es3:
                    tkbuf = _es3.enter_context(nc.sbuf_tensor("tkbuf", [128, 33 * 1024], BF16))
                    tkr = tkbuf[:].rearrange("p (t f) -> p t f", f=1024)
                    w_inkv = _es3.enter_context(nc.sbuf_tensor("w_inkv", [128, ND * 2048], BF16))
                    kvT = _es3.enter_context(nc.sbuf_tensor("kvT", [128, NPAIR * 704], F32))
                    kfp = _es3.enter_context(nc.sbuf_tensor("kfp", [128, SCH * DM], BF16))
                    vbf = _es3.enter_context(nc.sbuf_tensor("vbf", [128, 2 * SCH * DM], BF16))
                    _fA = {k_: _es3.enter_context(
                        nc.sbuf_tensor(f"fA_{k_}", [128, DM], BF16))
                        for k_ in ("a", "b", "c")}
                    fA_sc = {"sqa": _fA["a"], "pc": _fA["a"],
                             "sqb": _fA["b"], "pb": _fA["b"],
                             "ub": _fA["c"], "pa": _fA["c"]}
                    msb = _es3.enter_context(nc.sbuf_tensor("msb", [128, 2 * 1408], BF16))
                    ps_kv = _es3.enter_context(nc.psum_tensor("ps_kv", [128, 1024], F32))
                    ps_e1 = _es3.enter_context(nc.psum_tensor("ps_e1", [128, 3072], F32))
                    ps_tr = ps_e1[:].bitcast(BF16)
                    blk = _es3.enter_context(nc.Block())
                    snap_fence()
                    xA0 = xsem.n
                    xmk_xnT = {d: xA0 + 16 * (d + 1) for d in range(ND)}
                    gA0 = gsem.n
                    gmk_wkv = {d: gA0 + 16 * (d + 1) for d in range(ND)}
                    tA0 = tsem.n
                    tplan = {}
                    tc = tA0
                    for c in range(SCH):
                        tc += 1
                        tplan[f"kvpA0_{c}"] = tc
                        tc += 1
                        tplan[f"kvpB0_{c}"] = tc
                    for u in range(NSUP):
                        if u + 1 < NSUP:
                            for c in range(SCH):
                                tc += 1
                                tplan[f"kvpA{u + 1}_{c}"] = tc
                                tc += 1
                                tplan[f"kvpB{u + 1}_{c}"] = tc
                        for p in range(NPAIR):
                            tc += 1
                            tplan[f"e1{u}_{p}"] = tc
                    for p in range(NPAIR):
                        tc += 1
                        tplan[f"tr{p}"] = tc

                    xqA = XQ(vsem, ssem)
                    xqA.v(lambda e: e.memset(msb[:], 0.0))
                    for st_ in range(3):
                        xqA.v((lambda s_: lambda e: e.memset(
                            tkbuf[:, s_ * 11 * 1024:(s_ * 11 + 1) * 1024],
                            1.0))(st_))
                    xqA.mark("ones")

                    def emit_kvacc(q, u):
                        for p in range(NPAIR):
                            first = (u == 0)
                            eo = (p % 2) * 1536
                            for half in range(2):
                                rows = slice(0, 64) if half == 0 else slice(64, 128)
                                psap = ps_e1[rows, eo:eo + 1408].rearrange(
                                    "p (m f) -> p m f", f=128)[:, :,
                                                               half * 64:half * 64 + 64]
                                sbap = kvT[rows, p * 704:(p + 1) * 704].rearrange(
                                    "p (m k) -> p m k", k=64)
                                if first:
                                    fn = (lambda a_, b_: lambda e: e.tensor_copy(
                                        b_, a_))(psap, sbap)
                                else:
                                    fn = (lambda a_, b_: lambda e: e.tensor_tensor(
                                        b_, a_, b_, ALU.add))(psap, sbap)
                                q.v(fn, waits=[(tsem, tplan[f"e1{u}_{p}"])])
                            q.mark(f"kvacc{u}_{p}")

                    for u in range(NSUP):
                        vbo = (u % 2) * SCH * DM
                        prev_e1 = ([(tsem, tplan[f"e1{u - 1}_{NPAIR - 1}"])]
                                   if u > 0 else [])
                        vb_guard = ([(tsem, tplan[f"e1{u - 2}_{NPAIR - 1}"])]
                                    if u >= 2 else [])
                        for c in range(SCH):
                            xqA.v((lambda c_: lambda e: e._custom_dve(
                                CLIP, out=kfp[:, c_ * DM:(c_ + 1) * DM],
                                in0=ps_kv[:, 0:DM], s0=0.125, s1=1.0,
                                imm2=0.0))(c),
                                waits=[(tsem, tplan[f"kvpA{u}_{c}"])], drain=True)
                            xqA.mark(f"kevA{u}_{c}")
                            xqA.s((lambda c_, o_: lambda e: e.copy(
                                vbf[:, o_ + c_ * DM:o_ + (c_ + 1) * DM],
                                ps_kv[:, 0:DM]))(c, vbo),
                                waits=vb_guard + [(tsem, tplan[f"kvpB{u}_{c}"])])
                            xqA.mark(f"kvev{u}_{c}")
                        recB = None
                        if u >= 1:
                            recB = XQRec()
                            emit_kvacc(recB, u - 1)
                        for c in range(SCH):
                            ko = c * DM
                            sc = {k_: t_[:, 0:DM] for k_, t_ in fA_sc.items()}
                            i_ = 2 * u + c
                            tkg = ([(tsem, tplan[
                                f"e1{(i_ - 3) // 2}_{NPAIR - 1}"])]
                                if i_ >= 3 else [])
                            sset = i_ % 3
                            recA = XQRec()
                            emit_cheb(recA, kfp[:, ko:ko + DM], sc,
                                      (lambda s_: lambda m: tkbuf[
                                          :, (s_ * 11 + m) * 1024:
                                          (s_ * 11 + m + 1) * 1024])(sset),
                                      tw=(lambda g_: lambda m: g_)(tkg))
                            recA.mark(f"tk{u}_{c}")
                            xq_merge(xqA, recA, recB if c == 0 else None,
                                     ratio=1)
                    emit_kvacc(xqA, NSUP - 1)
                    for p in range(NPAIR):
                        bw = ([(tsem, tplan[f"tr{p - 2}"])] if p >= 2 else [])
                        for m in range(NM):
                            for half in range(2):
                                rows = slice(0, 64) if half == 0 else slice(64, 128)
                                h = 2 * p + half
                                src = kvT[rows, p * 704 + m * 64:p * 704 + (m + 1) * 64]
                                dstc = (p % 2) * 1408 + m * 128 + half * 64
                                xqA.v((lambda s_, d_, b_: lambda e: e.tensor_scalar(
                                    d_, s_, b_, None, ALU.mult))(
                                    src, msb[rows, dstc:dstc + 64],
                                    beta_s[rows, h * NM + m:h * NM + m + 1]),
                                    waits=bw)
                        xqA.mark(f"mb{p}")
                        xqA.s((lambda p_: lambda e: e.copy(
                            kv_bd[:, p_ * 1408:(p_ + 1) * 1408],
                            ps_tr[:, (p_ % 2) * 2048:(p_ % 2) * 2048 + 1408]))(p),
                            waits=[(tsem, tplan[f"tr{p}"])])
                        xqA.mark(f"kvbd{p}")

                    @blk.scalar
                    def _(scalar):
                        fence(scalar)
                        for d in range(ND):
                            xsem.inc(scalar.dma_start(
                                xnT[:, d * S:(d + 1) * S],
                                xn_h[:, d * 128:(d + 1) * 128], transpose=True), 16)
                        assert xsem.n == xmk_xnT[ND - 1]
                        xqA.run("s", scalar)

                    @blk.gpsimd
                    def _(gpsimd):
                        fence(gpsimd)
                        for d in range(ND):
                            gsem.inc(gpsimd.dma_start(
                                w_inkv[:, d * 2048:(d + 1) * 2048],
                                w_inT[d * 128:(d + 1) * 128, DM:DQKV]), 16)
                        assert gsem.n == gmk_wkv[ND - 1]

                    @blk.vector
                    def _(vector):
                        fence(vector)
                        xqA.run("v", vector)

                    @blk.tensor
                    def _(tensor):
                        fence(tensor)
                        tensor.wait_ge(xsem.h, xmk_xnT[ND - 1])
                        tensor.wait_ge(gsem.h, gmk_wkv[ND - 1])
                        xqA.wait_mark(tensor, "ones")

                        def emit_kvproj(u):
                            for c in range(SCH):
                                i = u * SCH + c
                                if i > 0:
                                    pu, pc = (u, c - 1) if c > 0 else (u - 1, SCH - 1)
                                    xqA.wait_mark(tensor, f"kvev{pu}_{pc}")
                                for half in range(2):
                                    if half == 1:
                                        xqA.wait_mark(tensor, f"kevA{u}_{c}")
                                    for d in range(ND):
                                        for jg in (2 * half, 2 * half + 1):
                                            mm = tensor.matmul(
                                                ps_kv[:, (jg % 2) * 512:
                                                      (jg % 2) * 512 + 512],
                                                xnT[:, d * S + i * 128:
                                                    d * S + (i + 1) * 128],
                                                w_inkv[:, d * 2048 + jg * 512:
                                                       d * 2048 + (jg + 1) * 512],
                                                start=(d == 0), stop=(d == ND - 1))
                                    tsem.inc(mm)
                                    ev = "kvpA" if half == 0 else "kvpB"
                                    assert tsem.n == tplan[f"{ev}{u}_{c}"]

                        emit_kvproj(0)
                        for u in range(NSUP):
                            if u + 1 < NSUP:
                                emit_kvproj(u + 1)
                            for p in range(NPAIR):
                                if p >= 2:
                                    xqA.wait_mark(tensor, f"kvacc{u}_{p - 2}")
                                elif u > 0:
                                    xqA.wait_mark(tensor,
                                                  f"kvacc{u - 1}_{NPAIR - 2 + p}")
                                eo = (p % 2) * 1536
                                for c in range(SCH):
                                    xqA.wait_mark(tensor, f"tk{u}_{c}")
                                    vbo = (u % 2) * SCH * DM
                                    lhs = vbf[:, vbo + c * DM + p * 128:
                                              vbo + c * DM + (p + 1) * 128]
                                    st = (c == 0)
                                    sp = (c == SCH - 1)
                                    sset = (2 * u + c) % 3
                                    for m0, mn in ((0, 4), (4, 4), (8, 3)):
                                        rhs = tkr[:, sset * 11 + m0:
                                                  sset * 11 + m0 + mn,
                                                  p * 128:(p + 1) * 128]
                                        mm = tensor.matmul(
                                            ps_e1[:, eo + m0 * 128:
                                                  eo + (m0 + mn) * 128],
                                            lhs, rhs, start=st, stop=sp)
                                tsem.inc(mm)
                                assert tsem.n == tplan[f"e1{u}_{p}"]
                        for p in range(NPAIR):
                            xqA.wait_mark(tensor, f"mb{p}")
                            if p >= 2:
                                xqA.wait_mark(tensor, f"kvbd{p - 2}")
                            to = (p % 2) * 2048
                            mo = (p % 2) * 1408
                            for m in range(NM):
                                tr = tensor.matmul(
                                    ps_tr[:, to + m * 128:to + (m + 1) * 128],
                                    msb[:, mo + m * 128:mo + (m + 1) * 128],
                                    identb[:],
                                    start=(m in (0, 8)), stop=(m in (7, 10)),
                                    is_transpose=True)
                            tsem.inc(tr)
                            assert tsem.n == tplan[f"tr{p}"]

                    if "kvt" in dbg_aps or "kvbd" in dbg_aps:
                        @blk.sync
                        def _(sync):
                            if "kvt" in dbg_aps:
                                for p in range(NPAIR):
                                    xqA.wait_mark(sync, f"kvacc{NSUP - 1}_{p}")
                                osem.inc(sync.dma_start(dbg_aps["kvt"], kvT[:]), 16)
                            if "kvbd" in dbg_aps:
                                for p in range(NPAIR):
                                    xqA.wait_mark(sync, f"kvbd{p}")
                                osem.inc(sync.dma_start(dbg_aps["kvbd"], kv_bd[:]), 16)

                with ExitStack() as _esCD:
                    atile = _esCD.enter_context(
                        nc.sbuf_tensor("atile", [128, NPAIR * S], BF16))
                    with ExitStack() as _es4:
                        onesb = _es4.enter_context(nc.sbuf_tensor("onesb", [128, 2048], BF16))
                        w_inq = _es4.enter_context(nc.sbuf_tensor("w_inq", [128, ND * DM], BF16))
                        qp = _es4.enter_context(nc.sbuf_tensor("qp", [128, S], BF16))
                        _fC = {k_: _es4.enter_context(
                            nc.sbuf_tensor(f"fC_{k_}", [128, S], BF16))
                            for k_ in ("a", "b", "c")}
                        fC_sc = {"sqa": _fC["a"], "pc": _fC["a"],
                                 "sqb": _fC["b"], "pb": _fC["b"],
                                 "ub": _fC["c"], "pa": _fC["c"]}
                        tqb = _es4.enter_context(nc.sbuf_tensor("tqb", [128, 20 * S], BF16))
                        ps_q = _es4.enter_context(nc.psum_tensor("ps_q", [128, 2048], F32))
                        ps_o = _es4.enter_context(nc.psum_tensor("ps_o", [128, 2048], F32))
                        blk = _es4.enter_context(nc.Block())
                        snap_fence()
                        gC0 = gsem.n
                        gmk_wq = {d: gC0 + 16 * (d + 1) for d in range(ND)}
                        tC0 = tsem.n
                        tplanC = {}
                        tc = tC0
                        tc += 1
                        tplanC["qp0"] = tc
                        tc += 1
                        tplanC["qp1"] = tc
                        for p in range(NPAIR):
                            if p + 2 < NPAIR:
                                tc += 1
                                tplanC[f"qp{p + 2}"] = tc
                            for m in range(NM):
                                tc += 1
                                tplanC[f"e2_{p}_{m}"] = tc

                        xqC = XQ(vsem, ssem)
                        xqC.v(lambda e: e.memset(onesb[:], 1.0))

                        def emit_aev(p):
                            ao = p * S
                            xqC.s((lambda p_, a_: lambda e: e.copy(
                                atile[:, a_:a_ + S], ps_o[:, 0:2048]))(p, ao),
                                  waits=[(tsem, tplanC[f"e2_{p}_{NM - 1}"])])
                            xqC.mark(f"aev{p}")
                            if "attn" in dbg_aps:
                                xqC.s((lambda p_, a_: lambda e: (st_ast.xfer(
                                    e, attn_h[p_ * 128:(p_ + 1) * 128, :],
                                    atile[:, a_:a_ + S]), e.nop())[1])(p, ao),
                                    drain=True)

                        for p in range(NPAIR):
                            xqC.v((lambda p_: lambda e: e._custom_dve(
                                CLIP, out=qp[:], in0=ps_q[:, 0:2048], s0=0.125,
                                s1=1.0, imm2=0.0))(p),
                                waits=[(tsem, tplanC[f"qp{p}"])], drain=True)
                            xqC.mark(f"qev{p}")
                            sc = {k_: t_[:] for k_, t_ in fC_sc.items()}

                            def tw(m, p_=p):
                                if p_ >= 2:
                                    return [(tsem, tplanC[f"e2_{p_ - 2}_{m}"])]
                                return []

                            emit_cheb(xqC, qp[:], sc,
                                      (lambda p_: lambda m: tqb[
                                          :, ((p_ % 2) * 10 + m - 1) * S:
                                          ((p_ % 2) * 10 + m) * S])(p),
                                      tw=tw,
                                      on_feat=(lambda m, p_=p: xqC.mark(f"tq{p_}_{m}")))
                            if p >= 1:
                                emit_aev(p - 1)
                        emit_aev(NPAIR - 1)

                        @blk.scalar
                        def _(scalar):
                            fence(scalar)
                            xqC.run("s", scalar)

                        @blk.gpsimd
                        def _(gpsimd):
                            fence(gpsimd)
                            for d in range(ND):
                                gsem.inc(gpsimd.dma_start(
                                    w_inq[:, d * DM:(d + 1) * DM],
                                    w_inT[d * 128:(d + 1) * 128, 0:DM]), 16)
                            assert gsem.n == gmk_wq[ND - 1]

                        @blk.vector
                        def _(vector):
                            fence(vector)
                            xqC.run("v", vector)

                        def emit_proj(tensor, p):
                            if p > 0:
                                xqC.wait_mark(tensor, f"qev{p - 1}")
                            for d in range(ND):
                                for sg in range(4):
                                    mm = tensor.matmul(
                                        ps_q[:, sg * 512:(sg + 1) * 512],
                                        w_inq[:, d * DM + p * 128:
                                              d * DM + (p + 1) * 128],
                                        xnT[:, d * S + sg * 512:d * S + (sg + 1) * 512],
                                        start=(d == 0), stop=(d == ND - 1))
                            tsem.inc(mm)
                            assert tsem.n == tplanC[f"qp{p}"]

                        @blk.tensor
                        def _(tensor):
                            fence(tensor)
                            tensor.wait_ge(gsem.h, gmk_wq[ND - 1])
                            emit_proj(tensor, 0)
                            emit_proj(tensor, 1)
                            for p in range(NPAIR):
                                if p + 2 < NPAIR:
                                    emit_proj(tensor, p + 2)
                                if p > 0:
                                    xqC.wait_mark(tensor, f"aev{p - 1}")
                                for m in range(NM):
                                    if m == 0:
                                        rhs_t, rof = onesb, 0
                                    else:
                                        xqC.wait_mark(tensor, f"tq{p}_{m}")
                                        rhs_t, rof = tqb, ((p % 2) * 10 + m - 1) * S
                                    for sg in range(4):
                                        mm = tensor.matmul(
                                            ps_o[:, sg * 512:(sg + 1) * 512],
                                            kv_bd[:, p * 1408 + m * 128:
                                                  p * 1408 + (m + 1) * 128],
                                            rhs_t[:, rof + sg * 512:rof + (sg + 1) * 512],
                                            start=(m == 0), stop=(m == NM - 1))
                                    tsem.inc(mm)
                                    assert tsem.n == tplanC[f"e2_{p}_{m}"]

                        if "attn" in dbg_aps:
                            @blk.sync
                            def _(sync):
                                for sm in st_ast.sems:
                                    sync.wait_ge(sm.h, sm.n)
                                osem.inc(sync.dma_start(dbg_aps["attn"], attn_h), 16)

                    if upto >= 3:
                        with ExitStack() as _es5:
                            w_out_s = _es5.enter_context(nc.sbuf_tensor("w_out_s", [128, ND * DM], BF16))
                            x1all = _es5.enter_context(nc.sbuf_tensor("x1all", [128, NSUB * DM], F32))
                            xint = _es5.enter_context(nc.sbuf_tensor("xint", [128, NSUB * DM], BF16))
                            rint = xint[:, 0:2 * DM]
                            xqt = xint[:, 2 * DM:4 * DM]
                            ps_w = _es5.enter_context(nc.psum_tensor("ps_w", [128, 2048], F32))
                            blk = _es5.enter_context(nc.Block())
                            snap_fence()
                            gD0 = gsem.n
                            gmk_x = xisem.n + 16
                            gmk_wo = {d: gD0 + 16 * (d + 1) for d in range(ND)}
                            xD0 = xsem.n
                            xq0_ = st_xq.i
                            tD0 = tsem.n
                            tplanD = {f"wo{i}": tD0 + i + 1 for i in range(NSUB)}

                            xqD = XQ(vsem, ssem)
                            for i in range(NSUB):
                                for ig in range(2):
                                    xqD.s((lambda i_, g_: lambda e: e.activation(
                                        x1all[:, i_ * DM + g_ * 512:i_ * DM + (g_ + 1) * 512],
                                        ps_w[:, (i_ % 2) * 1024 + g_ * 512:
                                             (i_ % 2) * 1024 + (g_ + 1) * 512],
                                        ACT.Copy,
                                        accum_out=st2[:, (64 if g_ else 0) + i_:
                                                      (64 if g_ else 0) + i_ + 1]))(i, ig),
                                        waits=[(tsem, tplanD[f"wo{i}"])])
                                xqD.mark(f"x1e{i}")
                                xqD.s((lambda i_: lambda e: dma_nop(
                                    e, xsem, x1_h[i_ * 128:(i_ + 1) * 128, :],
                                    x1all[:, i_ * DM:(i_ + 1) * DM]))(i), drain=True)
                                xqD.v((lambda i_: lambda e: e.reduce_max(
                                    st2[:, 80 + i_:81 + i_],
                                    x1all[:, i_ * DM:(i_ + 1) * DM],
                                    axis=AXX))(i))
                                xqD.v((lambda i_: lambda e: e.tensor_reduce(
                                    st2[:, 96 + i_:97 + i_],
                                    x1all[:, i_ * DM:(i_ + 1) * DM],
                                    AXX, ALU.min))(i))
                                xqD.s((lambda i_: lambda e: e.activation(
                                    rint[:, 0:DM],
                                    x1all[:, i_ * DM:(i_ + 1) * DM], ACT.Square,
                                    accum_out=st2[:, 16 + i_:17 + i_]))(i),
                                    drain=True)
                            xqD.v(lambda e: e.tensor_tensor(st2[:, 0:16], st2[:, 0:16],
                                                            st2[:, 64:80], ALU.add),
                                  drain=True)
                            xqD.v(lambda e: e.tensor_scalar(st2[:, 32:48], st2[:, 0:16],
                                                            1.0 / DM, None, ALU.mult), drain=True)
                            xqD.v(lambda e: e.tensor_tensor(st2[:, 48:64], st2[:, 32:48],
                                                            st2[:, 32:48], ALU.mult), drain=True)
                            xqD.v(lambda e: e.scalar_tensor_tensor(st2[:, 48:64], st2[:, 16:32],
                                                                   1.0 / DM, st2[:, 48:64],
                                                                   ALU.mult, ALU.subtract), drain=True)
                            xqD.s(lambda e: e.activation(st2[:, 48:64], st2[:, 48:64], ACT.Sqrt,
                                                         bias=wmean[:, 7:8]), drain=True)
                            xqD.v(lambda e: e.reciprocal(st2[:, 64:80], st2[:, 48:64]), drain=True)
                            xqD.v(lambda e: e.tensor_tensor(st2[:, 80:96], st2[:, 80:96],
                                                            st2[:, 32:48], ALU.subtract), drain=True)
                            xqD.v(lambda e: e.tensor_tensor(st2[:, 96:112], st2[:, 32:48],
                                                            st2[:, 96:112], ALU.subtract))
                            xqD.v(lambda e: e.tensor_tensor(st2[:, 112:128], st2[:, 80:96],
                                                            st2[:, 96:112], ALU.max), drain=True)
                            xqD.v(lambda e: e.tensor_tensor(st2[:, 112:128], st2[:, 112:128],
                                                            st2[:, 64:80], ALU.mult), drain=True)
                            xqD.v(lambda e: e.tensor_scalar(st2[:, 112:128], st2[:, 112:128],
                                                            EPS, None, ALU.max), drain=True)
                            xqD.v(lambda e: e.reciprocal(st3[:, 0:16], st2[:, 112:128]), drain=True)
                            xqD.v(lambda e: e.tensor_scalar(st3[:, 0:16], st3[:, 0:16], 127.0,
                                                            None, ALU.mult), drain=True)
                            xqD.v(lambda e: e.tensor_scalar(st3[:, 16:32], st2[:, 112:128],
                                                            1.0 / 127.0, None, ALU.mult))
                            xqD.v(lambda e: e.tensor_tensor(st2[:, 128:144], st2[:, 64:80],
                                                            st3[:, 0:16], ALU.mult), drain=True)
                            xqD.v(lambda e: e.scalar_tensor_tensor(st2[:, 144:160], st2[:, 32:48],
                                                                   -1.0, st2[:, 128:144],
                                                                   ALU.mult, ALU.mult), drain=True)
                            xqD.mark("ln2")
                            for i in range(NSUB):
                                xqD.v((lambda i_: lambda e: e._custom_dve(
                                    ROUND_AFFINE, out=rint[:, (i_ % 2) * DM:(i_ % 2) * DM + DM],
                                    in0=x1all[:, i_ * DM:(i_ + 1) * DM],
                                    s0=st2[:, 128 + i_:129 + i_], s1=st2[:, 144 + i_:145 + i_],
                                    imm2=MAGIC))(i), drain=(i == 0))
                                xqD.v((lambda i_: lambda e: e.tensor_scalar(
                                    xqt[:, (i_ % 2) * DM:(i_ % 2) * DM + DM],
                                    rint[:, (i_ % 2) * DM:(i_ % 2) * DM + DM],
                                    st3[:, 16 + i_:17 + i_], None, ALU.mult))(i), drain=True,
                                    waits=([st_xq.spec(xq0_ + i - 2)] if i >= 2 else []))
                                xqD.mark(f"xq{i}")
                                xqD.s((lambda i_: lambda e: (st_xq.xfer(
                                    e, xq_h[i_ * 128:(i_ + 1) * 128, :],
                                    xqt[:, (i_ % 2) * DM:(i_ % 2) * DM + DM]), e.nop())[1])(i))

                            @blk.gpsimd
                            def _(gpsimd):
                                fence(gpsimd)
                                xisem.inc(gpsimd.dma_start(
                                    xint[:], bass.AP(x_hd, 0, [[DM, 128],
                                                               [128 * DM, NSUB], [1, DM]])),
                                    16)
                                assert xisem.n == gmk_x
                                for d in range(ND):
                                    gsem.inc(gpsimd.dma_start(
                                        w_out_s[:, d * DM:(d + 1) * DM],
                                        w_outT_h[d * 128:(d + 1) * 128, :]), 16)

                            @blk.vector
                            def _(vector):
                                fence(vector)
                                xqD.run("v", vector)

                            @blk.scalar
                            def _(scalar):
                                fence(scalar)
                                xqD.run("s", scalar)

                            @blk.tensor
                            def _(tensor):
                                fence(tensor)
                                tensor.wait_ge(gsem.h, gmk_wo[ND - 1])
                                for i in range(NSUB):
                                    if i >= 2:
                                        xqD.wait_mark(tensor, f"x1e{i - 2}")
                                    tensor.wait_ge(xisem.h, gmk_x)
                                    for d in range(ND):
                                        for ig in range(2):
                                            tensor.matmul(
                                                ps_w[:, (i % 2) * 1024 + ig * 512:
                                                     (i % 2) * 1024 + (ig + 1) * 512],
                                                atile[:, d * S + i * 128:
                                                      d * S + (i + 1) * 128],
                                                w_out_s[:, d * DM + ig * 512:
                                                        d * DM + (ig + 1) * 512],
                                                start=(d == 0), stop=False)
                                    for ig in range(2):
                                        mm = tensor.matmul(
                                            ps_w[:, (i % 2) * 1024 + ig * 512:
                                                 (i % 2) * 1024 + (ig + 1) * 512],
                                            identb[:],
                                            xint[:, i * DM + ig * 512:i * DM + (ig + 1) * 512],
                                            start=False, stop=True)
                                    tsem.inc(mm)
                                    assert tsem.n == tplanD[f"wo{i}"]

                            if "x1" in dbg_aps or "xq" in dbg_aps:
                                @blk.sync
                                def _(sync):
                                    sync.wait_ge(xsem.h, xsem.n)
                                    for sm in st_xq.sems:
                                        sync.wait_ge(sm.h, sm.n)
                                    if "x1" in dbg_aps:
                                        osem.inc(sync.dma_start(dbg_aps["x1"], x1_h), 16)
                                    if "xq" in dbg_aps:
                                        osem.inc(sync.dma_start(dbg_aps["xq"], xq_h), 16)

        if upto >= 4:
            with ExitStack() as _es6:
                h_sb = _es6.enter_context(nc.sbuf_tensor("h_sb", [128, NCF * S], BF16))
                rmax = _es6.enter_context(nc.sbuf_tensor("rmax", [128, S], BF16))
                rmin = _es6.enter_context(nc.sbuf_tensor("rmin", [128, S], BF16))
                with ExitStack() as _es7:
                    xqT = _es7.enter_context(nc.sbuf_tensor("xqT", [128, ND * S], BF16))
                    w1f = _es7.enter_context(nc.sbuf_tensor("w1f", [128, 2 * DM], F32))
                    w1q = _es7.enter_context(nc.sbuf_tensor("w1q", [128, 2 * DM], BF16))
                    cva = _es7.enter_context(nc.sbuf_tensor("cva", [128, S], BF16))
                    cta = _es7.enter_context(nc.sbuf_tensor("cta", [128, S], BF16))
                    ctb = _es7.enter_context(nc.sbuf_tensor("ctb", [128, S], BF16))
                    ps_f = _es7.enter_context(nc.psum_tensor("ps_f", [128, 4096], F32))
                    blk = _es7.enter_context(nc.Block())
                    snap_fence()
                    xF0 = xsem.n
                    xmk_xqT = {d: xF0 + 16 * (d + 1) for d in range(ND)}
                    w10 = st_w1.i
                    tF0 = tsem.n
                    tplanF = {f"f{c}": tF0 + c + 1 for c in range(NCF)}

                    xqF = XQ(vsem, ssem)

                    def emit_w1q(c):
                        xqF.v((lambda c_: lambda e: e._custom_dve(
                            WQUANT, out=w1q[:, (c_ % 2) * DM:(c_ % 2) * DM + DM],
                            in0=w1f[:, (c_ % 2) * DM:(c_ % 2) * DM + DM],
                            s0=wmean[:, 0:1], s1=1.0, imm2=MAGIC))(c),
                            waits=[st_w1.spec(w10 + c)] +
                            ([(tsem, tplanF[f"f{c - 2}"])] if c >= 2 else []))
                        xqF.mark(f"w1q{c}")

                    def emit_rm(c, half=None, drain=False):
                        h_ = h_sb[:, c * S:(c + 1) * S]
                        halves = (0, 1) if half is None else (half,)
                        for hh in halves:
                            buf = rmax if hh == 0 else rmin
                            op = ALU.max if hh == 0 else ALU.min
                            if c == 0:
                                xqF.v((lambda h2, b_: lambda e: e.tensor_copy(
                                    b_[:], h2))(h_, buf), drain=drain)
                            else:
                                xqF.v((lambda h2, b_, o_: lambda e: e.tensor_tensor(
                                    b_[:], b_[:], h2, o_))(h_, buf, op),
                                    drain=drain)
                        if half in (None, 1):
                            xqF.mark(f"rm{c}")

                    emit_w1q(0)
                    for c in range(NCF):
                        if c >= 1:
                            emit_rm(c - 1, half=0, drain=True)
                        if c + 1 < NCF:
                            emit_w1q(c + 1)
                        ho = c * S
                        h_c = h_sb[:, ho:ho + S]
                        pr = (c % 2) * 2048
                        for sg in range(4):
                            xqF.s((lambda c_, g_, r_: lambda e: e.copy(
                                h_sb[:, c_ * S + g_ * 512:c_ * S + (g_ + 1) * 512],
                                ps_f[:, r_ + g_ * 512:r_ + (g_ + 1) * 512]))(c, sg, pr),
                                waits=[(tsem, tplanF[f"f{c}"])])
                        xqF.mark(f"hev{c}")
                        cw0 = convw[:, c * 3:c * 3 + 1]
                        cw1 = convw[:, c * 3 + 1:c * 3 + 2]
                        cw2 = convw[:, c * 3 + 2:c * 3 + 3]
                        cb = convb[:, c:c + 1]
                        xqF.v((lambda h_, w_, b_: lambda e: e.tensor_scalar(
                            cva[:], h_, w_, b_, ALU.mult, ALU.add))(h_c, cw1, cb),
                            drain=(c == 0))
                        xqF.v((lambda h_, w_: lambda e: e.tensor_scalar(
                            cta[:], h_, w_, None, ALU.mult))(h_c, cw0))
                        xqF.v((lambda h_, w_: lambda e: e.tensor_scalar(
                            ctb[:], h_, w_, None, ALU.mult))(h_c, cw2))
                        if c >= 1:
                            emit_rm(c - 1, half=1, drain=True)
                        xqF.v(lambda e: e.tensor_tensor(
                            cva[:, 1:S], cva[:, 1:S], cta[:, 0:S - 1], ALU.add),
                            drain=(c == 0))
                        xqF.v(lambda e: e.tensor_tensor(
                            cva[:, 0:S - 1], cva[:, 0:S - 1], ctb[:, 1:S],
                            ALU.add), drain=True)
                        xqF.s((lambda h_: lambda e: e.activation(
                            h_, cva[:], ACT.Silu))(h_c), drain=True)
                        xqF.mark(f"silu{c}")
                    emit_rm(NCF - 1)

                    @blk.scalar
                    def _(scalar):
                        fence(scalar)
                        for d in range(ND):
                            xsem.inc(scalar.dma_start(
                                xqT[:, d * S:(d + 1) * S],
                                xq_h[:, d * 128:(d + 1) * 128], transpose=True), 16)
                        assert xsem.n == xmk_xqT[ND - 1]
                        xqF.run("s", scalar)

                    @blk.sync
                    def _(sync):
                        fence(sync)
                        for c in range(NCF):
                            if c >= 2:
                                xqF.wait_mark(sync, f"w1q{c - 2}")
                            st_w1.xfer(sync,
                                       w1f[:, (c % 2) * DM:(c % 2) * DM + DM],
                                       w1T_h[c * 128:(c + 1) * 128, :])

                    @blk.vector
                    def _(vector):
                        fence(vector)
                        xqF.run("v", vector)

                    if "xqt" in dbg_aps:
                        @blk.sync
                        def _(sync):
                            sync.wait_ge(xsem.h, xmk_xqT[ND - 1])
                            osem.inc(sync.dma_start(dbg_aps["xqt"], xqT[:]), 16)

                    if "hs" in dbg_aps or "rmx" in dbg_aps:
                        @blk.sync
                        def _(sync):
                            for c in range(NCF):
                                xqF.wait_mark(sync, f"rm{c}")
                            if "hs" in dbg_aps:
                                osem.inc(sync.dma_start(dbg_aps["hs"], h_sb[:]), 16)
                            if "rmx" in dbg_aps:
                                osem.inc(sync.dma_start(dbg_aps["rmx"][0:128, 0:S],
                                                        rmax[:]), 16)
                                osem.inc(sync.dma_start(dbg_aps["rmx"][0:128, S:2 * S],
                                                        rmin[:]), 16)

                    @blk.tensor
                    def _(tensor):
                        fence(tensor)
                        tensor.wait_ge(xsem.h, xmk_xqT[ND - 1])
                        for c in range(NCF):
                            xqF.wait_mark(tensor, f"w1q{c}")
                            if c >= 2:
                                xqF.wait_mark(tensor, f"hev{c - 2}")
                            pr = (c % 2) * 2048
                            for d in range(ND):
                                for sg in range(4):
                                    mm = tensor.matmul(
                                        ps_f[:, pr + sg * 512:pr + (sg + 1) * 512],
                                        w1q[:, (c % 2) * DM + d * 128:
                                            (c % 2) * DM + (d + 1) * 128],
                                        xqT[:, d * S + sg * 512:d * S + (sg + 1) * 512],
                                        start=(d == 0), stop=(d == ND - 1))
                            tsem.inc(mm)
                            assert tsem.n == tplanF[f"f{c}"]

                with ExitStack() as _es8:
                    selb = _es8.enter_context(nc.sbuf_tensor("selb", [16, NSUB * 128], F32))
                    stile = _es8.enter_context(nc.sbuf_tensor("stile", [128, S], F32))
                    comb = _es8.enter_context(nc.sbuf_tensor("comb", [128, S], BF16))
                    sp16 = _es8.enter_context(nc.sbuf_tensor("sp16", [16, 128], F32))
                    ps_t = _es8.enter_context(nc.psum_tensor("ps_t", [128, 2048], BF16))
                    ps_s = _es8.enter_context(nc.psum_tensor("ps_s", [128, 2048], F32))
                    ps_sp = _es8.enter_context(nc.psum_tensor("ps_sp", [16, 128], F32))
                    blk = _es8.enter_context(nc.Block())
                    snap_fence()
                    dFq0 = dsem.n
                    dmk_sel = dFq0 + 16
                    xFq0 = xsem.n
                    xmk_hq = {c: xFq0 + 16 * (c + 1) for c in range(NCF)}
                    tFq0 = tsem.n
                    tplanQ = {"trs": tFq0 + 1, "spt": tFq0 + 2, "bc": tFq0 + 3}

                    xqQ = XQ(vsem, ssem)
                    xqQ.v(lambda e: e.scalar_tensor_tensor(comb[:], rmin[:], -1.0,
                                                           rmax[:], ALU.mult, ALU.max),
                          drain=True)
                    xqQ.mark("comb")
                    for j in range(NSUB):
                        xqQ.v((lambda j_: lambda e: e.reduce_max(
                            st2[:, j_:j_ + 1],
                            ps_t[:, j_ * 128:(j_ + 1) * 128], axis=AXX))(j),
                            waits=[(tsem, tplanQ["trs"])])
                    xqQ.mark("amax2")
                    xqQ.v(lambda e: e.tensor_scalar(st2[:, 0:16], st2[:, 0:16], EPS,
                                                    None, ALU.max), drain=True)
                    xqQ.v(lambda e: e.reciprocal(st2[:, 16:32], st2[:, 0:16]),
                          drain=True)
                    xqQ.v(lambda e: e.tensor_scalar(st2[:, 16:32], st2[:, 16:32],
                                                    127.0, None, ALU.mult), drain=True)
                    xqQ.v(lambda e: e.tensor_scalar(st2[:, 32:48], st2[:, 0:16],
                                                    1.0 / 127.0, wmean[:, 3:4],
                                                    ALU.mult, ALU.mult))
                    xqQ.mark("s2tok")
                    xqQ.s(lambda e: e.copy(sp16[:], ps_sp[:]),
                          waits=[(tsem, tplanQ["spt"])])
                    xqQ.mark("sp")
                    for g in range(4):
                        xqQ.v((lambda g_: lambda e: e.tensor_copy(
                            stile[:, g_ * 512:(g_ + 1) * 512],
                            ps_s[:, g_ * 512:(g_ + 1) * 512]))(g),
                            waits=[(tsem, tplanQ["bc"])])
                    xqQ.mark("stile")
                    for c in range(NCF):
                        h_c = h_sb[:, c * S:(c + 1) * S]
                        xqQ.v((lambda h_: lambda e: e._custom_dve(
                            ROUND_MUL, out=h_, in0=h_, in1=stile[:], s0=0.0,
                            imm2=MAGIC))(h_c), drain=True)
                        xqQ.mark(f"hq{c}")
                        xqQ.s((lambda c_, h_: lambda e: dma_nop(
                            e, xsem, hq_h[c_ * 128:(c_ + 1) * 128, :], h_))(c, h_c))

                    @blk.sync
                    def _(sync):
                        fence(sync)
                        dsem.inc(sync.dma_start(selb[:], sel_in), 16)
                        assert dsem.n == dmk_sel

                    @blk.vector
                    def _(vector):
                        fence(vector)
                        xqQ.run("v", vector)

                    @blk.scalar
                    def _(scalar):
                        fence(scalar)
                        xqQ.run("s", scalar)
                        assert xsem.n == xmk_hq[NCF - 1]

                    @blk.tensor
                    def _(tensor):
                        fence(tensor)
                        xqQ.wait_mark(tensor, "comb")
                        for j in range(NSUB):
                            tr = tensor.matmul(
                                ps_t[:, j * 128:(j + 1) * 128],
                                comb[:, j * 128:(j + 1) * 128], identb[:],
                                start=(j in (0, 8)), stop=(j in (7, 15)),
                                is_transpose=True)
                        tsem.inc(tr)
                        assert tsem.n == tplanQ["trs"]
                        xqQ.wait_mark(tensor, "s2tok")
                        tsem.inc(tensor.matmul(ps_sp[:], st2[:, 16:32], identf[:],
                                               start=True, stop=True,
                                               is_transpose=True))
                        assert tsem.n == tplanQ["spt"]
                        xqQ.wait_mark(tensor, "sp")
                        tensor.wait_ge(dsem.h, dmk_sel)
                        for j in range(NSUB):
                            mm = tensor.matmul(ps_s[:, j * 128:(j + 1) * 128],
                                               selb[0:16, j * 128:(j + 1) * 128],
                                               sp16[0:16, 0:128],
                                               start=(j % 4 == 0),
                                               stop=(j % 4 == 3))
                        tsem.inc(mm)
                        assert tsem.n == tplanQ["bc"]

                    if "hq" in dbg_aps or "stile" in dbg_aps:
                        @blk.sync
                        def _(sync):
                            sync.wait_ge(xsem.h, xsem.n)
                            if "hq" in dbg_aps:
                                osem.inc(sync.dma_start(dbg_aps["hq"], hq_h), 16)
                            if "stile" in dbg_aps:
                                xqQ.wait_mark(sync, "stile")
                                osem.inc(sync.dma_start(dbg_aps["stile"], stile[:]), 16)

        if upto >= 6:
            with ExitStack() as _es9:
                hqs = _es9.enter_context(nc.sbuf_tensor("hqs", [128, 4 * DFF], BF16))
                w2f = _es9.enter_context(nc.sbuf_tensor("w2f", [128, 16 * DM], F32))
                w2q = _es9.enter_context(nc.sbuf_tensor("w2q", [128, NCF * DM], BF16))
                x1t = _es9.enter_context(nc.sbuf_tensor("x1t", [128, 2 * DM], F32))
                outt = _es9.enter_context(nc.sbuf_tensor("outt", [128, 2 * DM], F32))
                ps_g = _es9.enter_context(nc.psum_tensor("ps_g", [128, 2048], F32))
                blk = _es9.enter_context(nc.Block())
                snap_fence()
                hq0 = st_hq.i
                x1t0 = st_x1t.i
                w20 = st_w2.i
                out0 = st_out.i
                tG0 = tsem.n
                tplanG = {f"g{i}": tG0 + i + 1 for i in range(NSUB)}

                xqG = XQ(vsem, ssem)
                for c in range(NCF):
                    xqG.v((lambda c_: lambda e: e._custom_dve(
                        WQUANT, out=w2q[:, c_ * DM:(c_ + 1) * DM],
                        in0=w2f[:, (c_ % 16) * DM:(c_ % 16) * DM + DM],
                        s0=wmean[:, 2:3], s1=1.0, imm2=MAGIC))(c),
                        waits=[st_w2.spec(w20 + c)], drain=True)
                    xqG.mark(f"w2q{c}")
                for i in range(NSUB):
                    for ig in range(2):
                        xqG.v((lambda i_, g_: lambda e: e.scalar_tensor_tensor(
                            outt[:, (i_ % 2) * DM + g_ * 512:
                                 (i_ % 2) * DM + (g_ + 1) * 512],
                            ps_g[:, (i_ % 2) * 1024 + g_ * 512:
                                 (i_ % 2) * 1024 + (g_ + 1) * 512],
                            st2[:, 32 + i_:33 + i_],
                            x1t[:, (i_ % 2) * DM + g_ * 512:
                                (i_ % 2) * DM + (g_ + 1) * 512],
                            ALU.mult, ALU.add))(i, ig),
                            waits=[(tsem, tplanG[f"g{i}"]),
                                   st_x1t.spec(x1t0 + i)] +
                            ([st_out.spec(out0 + i - 2)] if i >= 2 else []))
                    xqG.mark(f"oev{i}")
                    xqG.s((lambda i_: lambda e: (st_out.xfer(
                        e, out_hb[i_ * 128:(i_ + 1) * 128, :],
                        outt[:, (i_ % 2) * DM:(i_ % 2) * DM + DM]),
                        e.nop())[1])(i), drain=True)
                    xqG.mark(f"ost{i}")

                @blk.sync
                def _(sync):
                    fence(sync)
                    for i in range(NSUB):
                        if i >= 4:
                            sync.wait_ge(tsem.h, tplanG[f"g{i - 4}"])
                        if i >= 2:
                            sync.wait_ge(vsem.h, xqG.marks[f"oev{i - 2}"][0])
                        srcap = bass.AP(hq_hd, i * 128,
                                        [[S, 128], [128 * S, NCF], [1, 128]])
                        st_hq.xfer(sync, hqs[:, (i % 4) * DFF:
                                             (i % 4) * DFF + DFF], srcap)
                        st_x1t.xfer(sync,
                                    x1t[:, (i % 2) * DM:(i % 2) * DM + DM],
                                    x1_h[i * 128:(i + 1) * 128, :])
                    for sm in st_out.sems:
                        sync.wait_ge(sm.h, sm.n)

                @blk.vector
                def _(vector):
                    fence(vector)
                    xqG.run("v", vector)

                @blk.scalar
                def _(scalar):
                    fence(scalar)
                    for ci in range(NCF):
                        if ci >= 16:
                            scalar.wait_ge(vsem.h,
                                           xqG.marks[f"w2q{ci - 16}"][0])
                        st_w2.xfer(scalar,
                                   w2f[:, (ci % 16) * DM:(ci % 16) * DM + DM],
                                   w2T_h[ci * 128:(ci + 1) * 128, :])
                    xqG.run("s", scalar)
                    for sm in st_out.sems:
                        scalar.wait_ge(sm.h, sm.n)

                @blk.tensor
                def _(tensor):
                    fence(tensor)
                    for i in range(NSUB):
                        st_hq.wait_idx(tensor, hq0 + i)
                        if i >= 2:
                            xqG.wait_mark(tensor, f"oev{i - 2}")
                        for c in range(NCF):
                            xqG.wait_mark(tensor, f"w2q{c}")
                            for ig in range(2):
                                mm = tensor.matmul(
                                    ps_g[:, (i % 2) * 1024 + ig * 512:
                                         (i % 2) * 1024 + (ig + 1) * 512],
                                    hqs[:, (i % 4) * DFF + c * 128:
                                        (i % 4) * DFF + (c + 1) * 128],
                                    w2q[:, c * DM + ig * 512:c * DM + (ig + 1) * 512],
                                    start=(c == 0), stop=(c == NCF - 1))
                        tsem.inc(mm)
                        assert tsem.n == tplanG[f"g{i}"]

    mybir.codegen_inst_isa_subclasses(nc)
    return nc, dbg_aps


def _host_inputs(core_x, L0_attn_w_in, L0_attn_w_out, L0_attn_beta,
                 L0_ffn_w1, L0_ffn_conv_w, L0_ffn_conv_b, L0_ffn_w2):
    f32c = lambda a: np.ascontiguousarray(np.asarray(a, dtype=np.float32))
    w_inT = f32c(np.asarray(L0_attn_w_in).T)
    w_outT = f32c(np.asarray(L0_attn_w_out).T)
    w1T = f32c(np.asarray(L0_ffn_w1, dtype=np.float32)
               .reshape(NCF, 128, ND, 128).transpose(0, 3, 2, 1)
               .reshape(DFF, DM))
    w2T = f32c(np.asarray(L0_ffn_w2).T)
    convw = f32c(np.asarray(L0_ffn_conv_w).reshape(NCF, 128, 3)
                 .transpose(1, 0, 2).reshape(128, NCF * 3))
    convb = f32c(np.asarray(L0_ffn_conv_b).reshape(NCF, 128).T)
    beta_t = f32c(np.tile(np.asarray(L0_attn_beta).reshape(1, 176), (128, 1)))
    identb = np.eye(128).astype(bf16)
    identf = np.eye(128, dtype=np.float32)
    onesf = np.ones((128, 128), np.float32)
    sel = np.zeros((16, NSUB * 128), np.float32)
    for j in range(16):
        sel[j, j * 128:(j + 1) * 128] = 1.0
    base = {"w_inT": w_inT, "w_outT": w_outT, "w1T": w1T, "w2T": w2T,
            "convw": convw, "convb": convb, "beta_t": beta_t,
            "identb": identb, "identf": identf, "onesf": onesf, "sel": sel}
    maps = []
    for b in range(core_x.shape[0]):
        m = dict(base)
        m["x"] = np.ascontiguousarray(np.asarray(core_x[b], dtype=np.float32))
        maps.append(m)
    return maps


_BUILD_CACHE = {}


def kernel(L0_attn_x, L0_attn_w_in, L0_attn_w_out, L0_attn_beta,
           L0_ffn_w1, L0_ffn_conv_w, L0_ffn_conv_b, L0_ffn_w2):
    x = np.asarray(L0_attn_x, dtype=np.float32)
    nb = x.shape[0]
    if "nc" not in _BUILD_CACHE:
        _BUILD_CACHE["nc"] = build()[0]
    nc = _BUILD_CACHE["nc"]
    in_maps = _host_inputs(x, L0_attn_w_in, L0_attn_w_out, L0_attn_beta,
                           L0_ffn_w1, L0_ffn_conv_w, L0_ffn_conv_b, L0_ffn_w2)
    res = run_bass_kernel_spmd(nc, in_maps, core_ids=list(range(nb)))
    out = np.stack([np.asarray(r["out"], dtype=np.float32)
                    for r in res.results], axis=0)
    return out


# version 20 (final)
# speedup vs baseline: 1.0085x; 1.0085x over previous
import numpy as np
import ml_dtypes
import concourse.bass as bass
import concourse.mybir as mybir
from concourse.bass_utils import run_bass_kernel_spmd
import concourse.dve_ops as dve_ops_mod
from concourse.dve_op import DveOp
from concourse.dve_spec import Spec, Src0, Src1, C0, C1, C2, Zero, One, lower, maxx, minn

F32 = mybir.dt.float32
BF16 = mybir.dt.bfloat16
ALU = mybir.AluOpType
ACT = mybir.ActivationFunctionType
AXX = mybir.AxisListType.X
bf16 = ml_dtypes.bfloat16

MAGIC = float(1.5 * 2.0 ** 23)
S, DM, DQKV, DFF = 2048, 1024, 3072, 4096
NSUB = S // 128
ND = DM // 128
NPAIR = 8
NM = 11
NCF = DFF // 128
SCH = 2
NSUP = NSUB // SCH
EPS = 1e-5


def _register_ops():
    ver = "v3"

    def mk(name, body, ref):
        low = lower(Spec(body=body), ver=ver, perf_en=False, name=name)
        low.validate(ver)
        return DveOp(name, Spec(body=body), subdim=False,
                     uops_sha={ver: low.sha(ver)}, reference=ref)

    ra = mk("ANT_ROUND_AFFINE", ((Src0 * C0 + C1) + C2) - C2,
            lambda in0, in1, s0, s1, imm2: np.round(in0 * s0 + s1))
    rm = mk("ANT_ROUND_MUL", ((Src0 * Src1 + C0) + C2) - C2,
            lambda in0, in1, s0, s1, imm2: np.round(in0 * in1 + s0))
    wq = mk("ANT_WQUANT", minn(maxx(((Src0 * C0 + C2) - C2), Zero - One), One) * C1,
            lambda in0, in1, s0, s1, imm2:
            np.minimum(np.maximum(np.round(in0 * s0), -1.0), 1.0) * s1)
    cl = mk("ANT_CLIP", minn(maxx(Src0 * C0, Zero - One), One),
            lambda in0, in1, s0, s1, imm2:
            np.minimum(np.maximum(in0 * s0, -1.0), 1.0))
    mine = [ra, rm, wq, cl]
    have = {o.name for o in dve_ops_mod.OPS}
    newops = [o for o in dve_ops_mod.OPS if not o.name.startswith("DVEHASH")]
    newops += [o for o in mine if o.name not in have]
    dve_ops_mod.OPS[:] = newops
    dve_ops_mod._SUB_OPCODE_FOR_NAME.clear()
    dve_ops_mod._SUB_OPCODE_FOR_NAME.update(
        {o.name: dve_ops_mod._CUSTOM_DVE_ROW_BASE + i for i, o in enumerate(newops)})
    dve_ops_mod.CUSTOM_DVE_SPECS.update({o.name: o.spec for o in mine})
    dve_ops_mod.CUSTOM_DVE_REFS.update({o.name: o.ref for o in mine})
    return ra, rm, wq, cl


ROUND_AFFINE, ROUND_MUL, WQUANT, CLIP = _register_ops()


class Sem:
    def __init__(self, es, nc, name):
        self.h = es.enter_context(nc.semaphore(name))
        self.n = 0

    def inc(self, instr, k=1):
        instr.then_inc(self.h, k)
        self.n += k
        return instr


class Stream:

    def __init__(self, es, nc, name, ns=2):
        self.ns = ns
        self.sems = [Sem(es, nc, f"{name}_{k}") for k in range(ns)]
        self.i = 0

    def xfer(self, eng, dst, src, **kw):
        s = self.sems[self.i % self.ns]
        s.inc(eng.dma_start(dst, src, **kw), 16)
        self.i += 1

    def spec(self, idx):
        return (self.sems[idx % self.ns], 16 * (idx // self.ns + 1))

    def wait_idx(self, eng, idx):
        sem, cnt = self.spec(idx)
        eng.wait_ge(sem.h, cnt)


class XQ:

    def __init__(self, vsem, ssem):
        self.vsem, self.ssem = vsem, ssem
        self.vbase, self.sbase = vsem.n, ssem.n
        self.ops = []
        self.marks = {}

    def v(self, fn, drain=False, waits=()):
        self.ops.append(("v", fn, drain, list(waits)))

    def s(self, fn, drain=False, waits=()):
        self.ops.append(("s", fn, drain, list(waits)))

    def mark(self, name):
        nv = sum(1 for o in self.ops if o[0] == "v")
        ns = sum(1 for o in self.ops if o[0] == "s")
        self.marks[name] = (self.vbase + nv, self.sbase + ns)

    def wait_mark(self, eng, name):
        v_abs, s_abs = self.marks[name]
        if v_abs > 0:
            eng.wait_ge(self.vsem.h, v_abs)
        if s_abs > 0:
            eng.wait_ge(self.ssem.h, s_abs)

    def run(self, which, eng):
        nv, ns = self.vbase, self.sbase
        last_cross = -1
        last_waits = {}
        mysem = self.vsem if which == "v" else self.ssem
        othersem = self.ssem if which == "v" else self.vsem
        for e, fn, drain, waits in self.ops:
            if e != which:
                if e == "v":
                    nv += 1
                else:
                    ns += 1
                continue
            cross = ns if which == "v" else nv
            if cross > last_cross and cross > 0:
                eng.wait_ge(othersem.h, cross)
                last_cross = cross
            for sem, cnt in waits:
                if cnt > 0 and last_waits.get(id(sem), -1) < cnt:
                    eng.wait_ge(sem.h, cnt)
                    last_waits[id(sem)] = cnt
            if drain:
                eng.drain()
            instr = fn(eng)
            assert instr is not None, "XQ op fn must return an instruction"
            mysem.inc(instr)
            if which == "v":
                nv += 1
            else:
                ns += 1
        if which == "v":
            assert self.vsem.n == nv, (self.vsem.n, nv)
        else:
            assert self.ssem.n == ns, (self.ssem.n, ns)


def emit_cheb(xq, src, sc, out_of, tw=None, on_feat=None):
    sqa, sqb, ub = sc["sqa"], sc["sqb"], sc["ub"]
    pa, pb, pc = sc["pa"], sc["pb"], sc["pc"]
    w = tw if tw else (lambda m: ())
    nf = on_feat if on_feat else (lambda m: None)
    o2, o3, o4, o5 = out_of(2), out_of(3), out_of(4), out_of(5)
    xq.s(lambda e: e.copy(out_of(1), src), waits=w(1)); nf(1)
    xq.v(lambda e: e.tensor_tensor(sqa, src, src, ALU.mult), drain=True)
    xq.v(lambda e: e.tensor_scalar(o2, sqa, 2.0, -1.0, ALU.mult, ALU.add),
         drain=True, waits=w(2)); nf(2)
    xq.v(lambda e: e.tensor_scalar(ub, o2, 2.0, -1.0, ALU.mult, ALU.add),
         drain=True)
    xq.v(lambda e: e.tensor_tensor(o3, src, ub, ALU.mult), drain=True,
         waits=w(3)); nf(3)
    xq.v(lambda e: e.tensor_tensor(sqb, o2, o2, ALU.mult))
    xq.v(lambda e: e.tensor_scalar(o4, sqb, 2.0, -1.0, ALU.mult, ALU.add),
         drain=True, waits=w(4)); nf(4)
    xq.v(lambda e: e.tensor_tensor(pa, o3, o2, ALU.mult))
    xq.v(lambda e: e.scalar_tensor_tensor(o5, pa, 2.0, src, ALU.mult,
                                          ALU.subtract), drain=True,
         waits=w(5)); nf(5)
    xq.s(lambda e: e.activation(sqa, o3, ACT.Square))
    xq.v(lambda e: e.tensor_tensor(pb, o4, o3, ALU.mult))
    xq.v(lambda e: e.tensor_scalar(out_of(6), sqa, 2.0, -1.0, ALU.mult, ALU.add),
         waits=w(6)); nf(6)
    xq.s(lambda e: e.activation(ub, o4, ACT.Square))
    xq.v(lambda e: e.tensor_tensor(pc, o5, o4, ALU.mult))
    xq.v(lambda e: e.scalar_tensor_tensor(out_of(7), pb, 2.0, src, ALU.mult,
                                          ALU.subtract), drain=True,
         waits=w(7)); nf(7)
    xq.s(lambda e: e.activation(sqb, o5, ACT.Square))
    xq.v(lambda e: e.tensor_scalar(out_of(8), ub, 2.0, -1.0, ALU.mult, ALU.add),
         waits=w(8)); nf(8)
    xq.v(lambda e: e.scalar_tensor_tensor(out_of(9), pc, 2.0, src, ALU.mult,
                                          ALU.subtract), drain=True,
         waits=w(9)); nf(9)
    xq.v(lambda e: e.tensor_scalar(out_of(10), sqb, 2.0, -1.0, ALU.mult, ALU.add),
         waits=w(10)); nf(10)


class XQRec:

    def __init__(self):
        self.items = []

    def v(self, fn, drain=False, waits=()):
        self.items.append(("v", fn, drain, list(waits)))

    def s(self, fn, drain=False, waits=()):
        self.items.append(("s", fn, drain, list(waits)))

    def mark(self, name):
        self.items.append(("mark", name, None, None))


def xq_merge(xq, rec_a, rec_b, ratio=2):
    def emit(item):
        kind, x, drain, waits = item
        if kind == "mark":
            xq.mark(x)
        elif kind == "v":
            xq.v(x, drain=drain, waits=waits)
        else:
            xq.s(x, drain=drain, waits=waits)

    bi = 0
    bitems = rec_b.items if rec_b else []
    inserted = [0]

    def pop_b():
        nonlocal bi
        if bi >= len(bitems):
            return
        emit(bitems[bi])
        if bitems[bi][0] == "v":
            inserted[0] += 1
        bi += 1
        while bi < len(bitems) and bitems[bi][0] == "mark":
            emit(bitems[bi])
            bi += 1

    nv = 0
    for item in rec_a.items:
        emit(item)
        if item[0] == "v":
            inserted[0] = 0
            nv += 1
            if nv % ratio == 0:
                pop_b()
                pop_b()
    while bi < len(bitems):
        emit(bitems[bi])
        bi += 1


def dma_nop(eng, dma_sem, dst, src, **kw):
    dma_sem.inc(eng.dma_start(dst, src, **kw), 16)
    return eng.nop()


def build(dbg=(), upto=99):
    import os
    from contextlib import ExitStack
    upto = int(os.environ.get("ANT_UPTO", upto))
    dbg = set(dbg)
    nc = bass.Bass("TRN2", target_bir_lowering=False,
                   dynamic_dma_scratch_size=16384)

    x_hd = nc.dram_tensor("x", [S, DM], F32, kind="ExternalInput")
    w_inT_hd = nc.dram_tensor("w_inT", [DM, DQKV], F32, kind="ExternalInput")
    w_outT_hd = nc.dram_tensor("w_outT", [DM, DM], F32, kind="ExternalInput")
    w1T_hd = nc.dram_tensor("w1T", [DFF, DM], F32, kind="ExternalInput")
    w2T_hd = nc.dram_tensor("w2T", [DFF, DM], F32, kind="ExternalInput")
    convw_in = nc.dram_tensor("convw", [128, NCF * 3], F32, kind="ExternalInput").ap()
    convb_in = nc.dram_tensor("convb", [128, NCF], F32, kind="ExternalInput").ap()
    beta_in = nc.dram_tensor("beta_t", [128, 176], F32, kind="ExternalInput").ap()
    identb_in = nc.dram_tensor("identb", [128, 128], BF16, kind="ExternalInput").ap()
    identf_in = nc.dram_tensor("identf", [128, 128], F32, kind="ExternalInput").ap()
    onesf_in = nc.dram_tensor("onesf", [128, 128], F32, kind="ExternalInput").ap()
    sel_in = nc.dram_tensor("sel", [16, NSUB * 128], F32, kind="ExternalInput").ap()
    x_in = x_hd.ap()
    w_inT = w_inT_hd.ap()
    w_outT_h = w_outT_hd.ap()
    w1T_h = w1T_hd.ap()
    w2T_h = w2T_hd.ap()
    out_hb = nc.dram_tensor("out", [S, DM], F32, kind="ExternalOutput").ap()
    xn_h = nc.dram_tensor("xn_h", [S, DM], BF16).ap()
    xq_h = nc.dram_tensor("xq_h", [S, DM], BF16).ap()
    x1_h = nc.dram_tensor("x1_h", [S, DM], F32).ap()
    attn_h = nc.dram_tensor("attn_h", [DM, S], BF16).ap()
    hq_hd = nc.dram_tensor("hq_h", [DFF, S], BF16)
    hq_h = hq_hd.ap()

    dbg_aps = {}
    for name, shape, dt in [("xn", [S, DM], BF16), ("kvt", [128, NPAIR * 704], F32),
                            ("kvbd", [128, NPAIR * 1408], BF16),
                            ("attn", [DM, S], BF16),
                            ("x1", [S, DM], F32), ("xq", [S, DM], BF16),
                            ("hq", [DFF, S], BF16), ("wm", [128, 8], F32),
                            ("stile", [128, S], F32), ("hs", [128, NCF * S], BF16),
                            ("rmx", [128, 2 * S], BF16),
                            ("xqt", [128, ND * S], BF16),
                            ("w1qd", [128, NCF * DM], BF16)]:
        if name in dbg:
            dbg_aps[name] = nc.dram_tensor("dbg_" + name, shape, dt,
                                           kind="ExternalOutput").ap()

    es = ExitStack()
    with es:
        identb = es.enter_context(nc.sbuf_tensor("identb_s", [128, 128], BF16))
        identf = es.enter_context(nc.sbuf_tensor("identf_s", [128, 128], F32))
        onesf = es.enter_context(nc.sbuf_tensor("onesf_s", [128, 128], F32))
        beta_s = es.enter_context(nc.sbuf_tensor("beta_s", [128, 176], F32))
        convw = es.enter_context(nc.sbuf_tensor("convw_s", [128, NCF * 3], F32))
        convb = es.enter_context(nc.sbuf_tensor("convb_s", [128, NCF], F32))
        st2 = es.enter_context(nc.sbuf_tensor("st2", [128, 160], F32))
        st3 = es.enter_context(nc.sbuf_tensor("st3", [128, 64], F32))
        wmean = es.enter_context(nc.sbuf_tensor("wmean", [128, 8], F32))
        wm1 = es.enter_context(nc.sbuf_tensor("wm1", [1, 4], F32))

        dsem = Sem(es, nc, "dsem")
        gsem = Sem(es, nc, "gsem")
        xsem = Sem(es, nc, "xsem")
        tsem = Sem(es, nc, "tsem")
        vsem = Sem(es, nc, "vsem")
        ssem = Sem(es, nc, "ssem")
        osem = Sem(es, nc, "osem")
        xisem = Sem(es, nc, "xisem")
        st_wabs = Stream(es, nc, "stwab", ns=8)
        st_xn = Stream(es, nc, "stxn")
        st_ast = Stream(es, nc, "stast")
        st_xq = Stream(es, nc, "stxq")
        st_w1 = Stream(es, nc, "stw1")
        st_w2 = Stream(es, nc, "stw2", ns=16)
        st_hq = Stream(es, nc, "sthq", ns=4)
        st_x1t = Stream(es, nc, "stx1t")
        st_out = Stream(es, nc, "stout")
        streams = [st_wabs, st_xn, st_ast, st_xq, st_w1, st_w2, st_hq,
                   st_x1t, st_out]
        allsems = ([dsem, gsem, xsem, tsem, vsem, ssem, osem, xisem]
                   + [s for st in streams for s in st.sems])

        _fence_counts = [{}]

        def snap_fence():
            _fence_counts[0] = {id(sem): sem.n for sem in allsems}

        def fence(eng):
            for sem in allsems:
                cnt = _fence_counts[0].get(id(sem), 0)
                if cnt > 0:
                    eng.wait_ge(sem.h, cnt)

        with ExitStack() as _es1:
            xall = _es1.enter_context(nc.sbuf_tensor("xall", [128, NSUB * DM], F32))
            wabs = _es1.enter_context(nc.sbuf_tensor("wabs", [128, 8 * DM], F32))
            sqdump = _es1.enter_context(nc.sbuf_tensor("sqdump", [128, DFF], F32))
            wpart = _es1.enter_context(nc.sbuf_tensor("wpart", [128, 64], F32))
            xnt = _es1.enter_context(nc.sbuf_tensor("xnt", [128, 2 * DM], BF16))
            ps_m = _es1.enter_context(nc.psum_tensor("ps_m", [128, 1536], F32))
            blk = _es1.enter_context(nc.Block())
            snap_fence()
            d0 = dsem.n
            dmk_all = d0 + 16 * 6 + 16 * NSUB
            t0 = tsem.n

            xq0 = XQ(vsem, ssem)
            xq0.v(lambda e: e.memset(wmean[:], 0.0))
            xq0.v(lambda e: e.memset(wmean[:, 7:8], EPS), drain=True)
            ci = 0
            for j in range(2):
                for i in range(DFF // 128):
                    src = wabs[:, (ci % 8) * DM:(ci % 8) * DM + DM]
                    col = wpart[:, j * 32 + i:j * 32 + i + 1]
                    xq0.s((lambda s_, c_: lambda e: e.activation(
                        sqdump[:, 0:DM], s_, ACT.Abs, accum_out=c_))(src, col),
                        waits=[st_wabs.spec(ci)], drain=True)
                    xq0.mark(f"wa{ci}")
                    ci += 1
            xq0.mark("wabs")
            for i in range(NSUB):
                xq0.v((lambda i_: lambda e: e.reduce_sum(
                    st2[:, i_:i_ + 1], xall[:, i_ * DM:(i_ + 1) * DM], axis=AXX))(i),
                    waits=[(dsem, dmk_all)])
            for i in range(NSUB):
                xq0.s((lambda i_: lambda e: e.activation(
                    sqdump[:, 0:DM], xall[:, i_ * DM:(i_ + 1) * DM], ACT.Square,
                    accum_out=st2[:, 16 + i_:17 + i_]))(i),
                    waits=[(dsem, dmk_all)], drain=True)
            xq0.v(lambda e: e.tensor_scalar(st2[:, 32:48], st2[:, 0:16],
                                            1.0 / DM, None, ALU.mult), drain=True)
            xq0.v(lambda e: e.tensor_tensor(st2[:, 48:64], st2[:, 32:48],
                                            st2[:, 32:48], ALU.mult), drain=True)
            xq0.v(lambda e: e.scalar_tensor_tensor(st2[:, 48:64], st2[:, 16:32],
                                                   1.0 / DM, st2[:, 48:64],
                                                   ALU.mult, ALU.subtract), drain=True)
            xq0.s(lambda e: e.activation(st2[:, 48:64], st2[:, 48:64], ACT.Sqrt,
                                         bias=wmean[:, 7:8]), drain=True)
            xq0.v(lambda e: e.reciprocal(st2[:, 64:80], st2[:, 48:64]), drain=True)
            xq0.v(lambda e: e.reduce_sum(st3[:, 0:1], wpart[:, 0:32], axis=AXX),
                  drain=True)
            xq0.v(lambda e: e.reduce_sum(st3[:, 1:2], wpart[:, 32:64], axis=AXX))
            xq0.mark("wsums")
            xn0 = st_xn.i
            for i in range(NSUB):
                xq0.v((lambda i_: lambda e: e.tensor_scalar(
                    xnt[:, (i_ % 2) * DM:(i_ % 2) * DM + DM],
                    xall[:, i_ * DM:(i_ + 1) * DM],
                    st2[:, 32 + i_:33 + i_], st2[:, 64 + i_:65 + i_],
                    ALU.subtract, ALU.mult))(i), drain=True,
                    waits=([st_xn.spec(xn0 + i - 2)] if i >= 2 else []))
                xq0.s((lambda i_: lambda e: (st_xn.xfer(
                    e, xn_h[i_ * 128:(i_ + 1) * 128, :],
                    xnt[:, (i_ % 2) * DM:(i_ % 2) * DM + DM]), e.nop())[1])(i))
            xq0.mark("xnst")
            xq0.v(lambda e: e.tensor_scalar(wm1[0:1, 0:1], ps_m[0:1, 0:1],
                                            1.0 / (DM * DFF), EPS, ALU.mult,
                                            ALU.max), waits=[(tsem, t0 + 1)])
            xq0.v(lambda e: e.tensor_scalar(wm1[0:1, 1:2], ps_m[0:1, 512:513],
                                            1.0 / (DM * DFF), EPS, ALU.mult,
                                            ALU.max))
            xq0.mark("wm1")
            xq0.v(lambda e: e.tensor_copy(wmean[:, 4:6], ps_m[:, 1024:1026]),
                  waits=[(tsem, t0 + 2)], drain=True)
            xq0.v(lambda e: e.tensor_copy(wmean[:, 1:2], wmean[:, 4:5]), drain=True)
            xq0.v(lambda e: e.tensor_copy(wmean[:, 3:4], wmean[:, 5:6]))
            xq0.v(lambda e: e.reciprocal(wmean[:, 0:1], wmean[:, 4:5]), drain=True)
            xq0.v(lambda e: e.reciprocal(wmean[:, 2:3], wmean[:, 5:6]))
            xq0.mark("wmean")
            xq0.v(lambda e: e.tensor_scalar(convw[:], convw[:], wmean[:, 1:2],
                                            None, ALU.mult),
                  drain=True, waits=[(dsem, dmk_all)])
            xq0.mark("convfold")

            @blk.sync
            def _(sync):
                for dst, src in [(identb[:], identb_in), (identf[:], identf_in),
                                 (onesf[:], onesf_in), (beta_s[:], beta_in),
                                 (convw[:], convw_in), (convb[:], convb_in)]:
                    dsem.inc(sync.dma_start(dst, src), 16)
                for i in range(NSUB):
                    dsem.inc(sync.dma_start(xall[:, i * DM:(i + 1) * DM],
                                            x_in[i * 128:(i + 1) * 128, :]), 16)
                assert dsem.n == dmk_all

            @blk.gpsimd
            def _(gpsimd):
                ci = 0
                for j in range(2):
                    wsrc = w1T_h if j == 0 else w2T_h
                    for i in range(DFF // 128):
                        if ci >= 8:
                            xq0.wait_mark(gpsimd, f"wa{ci - 8}")
                        st_wabs.xfer(gpsimd,
                                     wabs[:, (ci % 8) * DM:(ci % 8) * DM + DM],
                                     wsrc[i * 128:(i + 1) * 128, :])
                        ci += 1

            @blk.vector
            def _(vector):
                xq0.run("v", vector)

            @blk.scalar
            def _(scalar):
                xq0.run("s", scalar)

            @blk.tensor
            def _(tensor):
                xq0.wait_mark(tensor, "wsums")
                tensor.matmul(ps_m[0:1, 0:1], onesf[:, 0:1], st3[:, 0:1],
                              start=True, stop=True)
                tsem.inc(tensor.matmul(ps_m[0:1, 512:513], onesf[:, 0:1],
                                       st3[:, 1:2], start=True, stop=True))
                xq0.wait_mark(tensor, "wm1")
                tsem.inc(tensor.matmul(ps_m[:, 1024:1026], onesf[0:1, 0:128],
                                       wm1[0:1, 0:2], start=True, stop=True))
                assert tsem.n == t0 + 2

            if "wm" in dbg_aps or "xn" in dbg_aps:
                @blk.sync
                def _(sync):
                    if "wm" in dbg_aps:
                        xq0.wait_mark(sync, "wmean")
                        osem.inc(sync.dma_start(dbg_aps["wm"], wmean[:]), 16)
                    if "xn" in dbg_aps:
                        for sm in st_xn.sems:
                            sync.wait_ge(sm.h, sm.n)
                        osem.inc(sync.dma_start(dbg_aps["xn"], xn_h), 16)

        if upto >= 1:
            with ExitStack() as _es2:
                kv_bd = _es2.enter_context(nc.sbuf_tensor("kv_bd", [128, NPAIR * 1408], BF16))
                xnT = _es2.enter_context(nc.sbuf_tensor("xnT", [128, ND * S], BF16))
                with ExitStack() as _es3:
                    tkbuf = _es3.enter_context(nc.sbuf_tensor("tkbuf", [128, 33 * 1024], BF16))
                    tkr = tkbuf[:].rearrange("p (t f) -> p t f", f=1024)
                    w_inkv = _es3.enter_context(nc.sbuf_tensor("w_inkv", [128, ND * 2048], BF16))
                    kvT = _es3.enter_context(nc.sbuf_tensor("kvT", [128, NPAIR * 704], F32))
                    kfp = _es3.enter_context(nc.sbuf_tensor("kfp", [128, SCH * DM], BF16))
                    vbf = _es3.enter_context(nc.sbuf_tensor("vbf", [128, 2 * SCH * DM], BF16))
                    _fA = {k_: _es3.enter_context(
                        nc.sbuf_tensor(f"fA_{k_}", [128, DM], BF16))
                        for k_ in ("a", "b", "c")}
                    fA_sc = {"sqa": _fA["a"], "pc": _fA["a"],
                             "sqb": _fA["b"], "pb": _fA["b"],
                             "ub": _fA["c"], "pa": _fA["c"]}
                    msb = _es3.enter_context(nc.sbuf_tensor("msb", [128, 2 * 1408], BF16))
                    ps_kv = _es3.enter_context(nc.psum_tensor("ps_kv", [128, 1024], F32))
                    ps_e1 = _es3.enter_context(nc.psum_tensor("ps_e1", [128, 3072], F32))
                    ps_tr = ps_e1[:].bitcast(BF16)
                    blk = _es3.enter_context(nc.Block())
                    snap_fence()
                    xA0 = xsem.n
                    xmk_xnT = {d: xA0 + 16 * (d + 1) for d in range(ND)}
                    gA0 = gsem.n
                    gmk_wkv = {d: gA0 + 16 * (d + 1) for d in range(ND)}
                    tA0 = tsem.n
                    tplan = {}
                    tc = tA0
                    for c in range(SCH):
                        tc += 1
                        tplan[f"kvpA0_{c}"] = tc
                        tc += 1
                        tplan[f"kvpB0_{c}"] = tc
                    for u in range(NSUP):
                        if u + 1 < NSUP:
                            for c in range(SCH):
                                tc += 1
                                tplan[f"kvpA{u + 1}_{c}"] = tc
                                tc += 1
                                tplan[f"kvpB{u + 1}_{c}"] = tc
                        for p in range(NPAIR):
                            tc += 1
                            tplan[f"e1{u}_{p}"] = tc
                    for p in range(NPAIR):
                        tc += 1
                        tplan[f"tr{p}"] = tc

                    xqA = XQ(vsem, ssem)
                    xqA.v(lambda e: e.memset(msb[:], 0.0))
                    for st_ in range(3):
                        xqA.v((lambda s_: lambda e: e.memset(
                            tkbuf[:, s_ * 11 * 1024:(s_ * 11 + 1) * 1024],
                            1.0))(st_))
                    xqA.mark("ones")

                    def emit_kvacc(q, u):
                        for p in range(NPAIR):
                            first = (u == 0)
                            eo = (p % 2) * 1536
                            for half in range(2):
                                rows = slice(0, 64) if half == 0 else slice(64, 128)
                                psap = ps_e1[rows, eo:eo + 1408].rearrange(
                                    "p (m f) -> p m f", f=128)[:, :,
                                                               half * 64:half * 64 + 64]
                                sbap = kvT[rows, p * 704:(p + 1) * 704].rearrange(
                                    "p (m k) -> p m k", k=64)
                                if first:
                                    fn = (lambda a_, b_: lambda e: e.tensor_copy(
                                        b_, a_))(psap, sbap)
                                else:
                                    fn = (lambda a_, b_: lambda e: e.tensor_tensor(
                                        b_, a_, b_, ALU.add))(psap, sbap)
                                q.v(fn, waits=[(tsem, tplan[f"e1{u}_{p}"])])
                            q.mark(f"kvacc{u}_{p}")

                    for u in range(NSUP):
                        vbo = (u % 2) * SCH * DM
                        prev_e1 = ([(tsem, tplan[f"e1{u - 1}_{NPAIR - 1}"])]
                                   if u > 0 else [])
                        vb_guard = ([(tsem, tplan[f"e1{u - 2}_{NPAIR - 1}"])]
                                    if u >= 2 else [])
                        for c in range(SCH):
                            xqA.v((lambda c_: lambda e: e._custom_dve(
                                CLIP, out=kfp[:, c_ * DM:(c_ + 1) * DM],
                                in0=ps_kv[:, 0:DM], s0=0.125, s1=1.0,
                                imm2=0.0))(c),
                                waits=[(tsem, tplan[f"kvpA{u}_{c}"])], drain=True)
                            xqA.mark(f"kevA{u}_{c}")
                            xqA.s((lambda c_, o_: lambda e: e.copy(
                                vbf[:, o_ + c_ * DM:o_ + (c_ + 1) * DM],
                                ps_kv[:, 0:DM]))(c, vbo),
                                waits=vb_guard + [(tsem, tplan[f"kvpB{u}_{c}"])])
                            xqA.mark(f"kvev{u}_{c}")
                        recB = None
                        if u >= 1:
                            recB = XQRec()
                            emit_kvacc(recB, u - 1)
                        for c in range(SCH):
                            ko = c * DM
                            sc = {k_: t_[:, 0:DM] for k_, t_ in fA_sc.items()}
                            i_ = 2 * u + c
                            tkg = ([(tsem, tplan[
                                f"e1{(i_ - 3) // 2}_{NPAIR - 1}"])]
                                if i_ >= 3 else [])
                            sset = i_ % 3
                            recA = XQRec()
                            emit_cheb(recA, kfp[:, ko:ko + DM], sc,
                                      (lambda s_: lambda m: tkbuf[
                                          :, (s_ * 11 + m) * 1024:
                                          (s_ * 11 + m + 1) * 1024])(sset),
                                      tw=(lambda g_: lambda m: g_)(tkg))
                            recA.mark(f"tk{u}_{c}")
                            xq_merge(xqA, recA, recB if c == 0 else None,
                                     ratio=1)
                    emit_kvacc(xqA, NSUP - 1)
                    for p in range(NPAIR):
                        bw = ([(tsem, tplan[f"tr{p - 2}"])] if p >= 2 else [])
                        for m in range(NM):
                            for half in range(2):
                                rows = slice(0, 64) if half == 0 else slice(64, 128)
                                h = 2 * p + half
                                src = kvT[rows, p * 704 + m * 64:p * 704 + (m + 1) * 64]
                                dstc = (p % 2) * 1408 + m * 128 + half * 64
                                xqA.v((lambda s_, d_, b_: lambda e: e.tensor_scalar(
                                    d_, s_, b_, None, ALU.mult))(
                                    src, msb[rows, dstc:dstc + 64],
                                    beta_s[rows, h * NM + m:h * NM + m + 1]),
                                    waits=bw)
                        xqA.mark(f"mb{p}")
                        xqA.s((lambda p_: lambda e: e.copy(
                            kv_bd[:, p_ * 1408:(p_ + 1) * 1408],
                            ps_tr[:, (p_ % 2) * 2048:(p_ % 2) * 2048 + 1408]))(p),
                            waits=[(tsem, tplan[f"tr{p}"])])
                        xqA.mark(f"kvbd{p}")

                    @blk.scalar
                    def _(scalar):
                        fence(scalar)
                        for d in range(ND):
                            xsem.inc(scalar.dma_start(
                                xnT[:, d * S:(d + 1) * S],
                                xn_h[:, d * 128:(d + 1) * 128], transpose=True), 16)
                        assert xsem.n == xmk_xnT[ND - 1]
                        xqA.run("s", scalar)

                    @blk.gpsimd
                    def _(gpsimd):
                        fence(gpsimd)
                        for d in range(ND):
                            gsem.inc(gpsimd.dma_start(
                                w_inkv[:, d * 2048:(d + 1) * 2048],
                                w_inT[d * 128:(d + 1) * 128, DM:DQKV]), 16)
                        assert gsem.n == gmk_wkv[ND - 1]

                    @blk.vector
                    def _(vector):
                        fence(vector)
                        xqA.run("v", vector)

                    @blk.tensor
                    def _(tensor):
                        fence(tensor)
                        tensor.wait_ge(xsem.h, xmk_xnT[ND - 1])
                        tensor.wait_ge(gsem.h, gmk_wkv[ND - 1])
                        xqA.wait_mark(tensor, "ones")

                        def emit_kvproj(u):
                            for c in range(SCH):
                                i = u * SCH + c
                                if i > 0:
                                    pu, pc = (u, c - 1) if c > 0 else (u - 1, SCH - 1)
                                    xqA.wait_mark(tensor, f"kvev{pu}_{pc}")
                                for half in range(2):
                                    if half == 1:
                                        xqA.wait_mark(tensor, f"kevA{u}_{c}")
                                    for d in range(ND):
                                        for jg in (2 * half, 2 * half + 1):
                                            mm = tensor.matmul(
                                                ps_kv[:, (jg % 2) * 512:
                                                      (jg % 2) * 512 + 512],
                                                xnT[:, d * S + i * 128:
                                                    d * S + (i + 1) * 128],
                                                w_inkv[:, d * 2048 + jg * 512:
                                                       d * 2048 + (jg + 1) * 512],
                                                start=(d == 0), stop=(d == ND - 1))
                                    tsem.inc(mm)
                                    ev = "kvpA" if half == 0 else "kvpB"
                                    assert tsem.n == tplan[f"{ev}{u}_{c}"]

                        emit_kvproj(0)
                        for u in range(NSUP):
                            if u + 1 < NSUP:
                                emit_kvproj(u + 1)
                            for p in range(NPAIR):
                                if p >= 2:
                                    xqA.wait_mark(tensor, f"kvacc{u}_{p - 2}")
                                elif u > 0:
                                    xqA.wait_mark(tensor,
                                                  f"kvacc{u - 1}_{NPAIR - 2 + p}")
                                eo = (p % 2) * 1536
                                for c in range(SCH):
                                    xqA.wait_mark(tensor, f"tk{u}_{c}")
                                    vbo = (u % 2) * SCH * DM
                                    lhs = vbf[:, vbo + c * DM + p * 128:
                                              vbo + c * DM + (p + 1) * 128]
                                    st = (c == 0)
                                    sp = (c == SCH - 1)
                                    sset = (2 * u + c) % 3
                                    for m0, mn in ((0, 4), (4, 4), (8, 3)):
                                        rhs = tkr[:, sset * 11 + m0:
                                                  sset * 11 + m0 + mn,
                                                  p * 128:(p + 1) * 128]
                                        mm = tensor.matmul(
                                            ps_e1[:, eo + m0 * 128:
                                                  eo + (m0 + mn) * 128],
                                            lhs, rhs, start=st, stop=sp)
                                tsem.inc(mm)
                                assert tsem.n == tplan[f"e1{u}_{p}"]
                        for p in range(NPAIR):
                            xqA.wait_mark(tensor, f"mb{p}")
                            if p >= 2:
                                xqA.wait_mark(tensor, f"kvbd{p - 2}")
                            to = (p % 2) * 2048
                            mo = (p % 2) * 1408
                            for m in range(NM):
                                tr = tensor.matmul(
                                    ps_tr[:, to + m * 128:to + (m + 1) * 128],
                                    msb[:, mo + m * 128:mo + (m + 1) * 128],
                                    identb[:],
                                    start=(m in (0, 8)), stop=(m in (7, 10)),
                                    is_transpose=True)
                            tsem.inc(tr)
                            assert tsem.n == tplan[f"tr{p}"]

                    if "kvt" in dbg_aps or "kvbd" in dbg_aps:
                        @blk.sync
                        def _(sync):
                            if "kvt" in dbg_aps:
                                for p in range(NPAIR):
                                    xqA.wait_mark(sync, f"kvacc{NSUP - 1}_{p}")
                                osem.inc(sync.dma_start(dbg_aps["kvt"], kvT[:]), 16)
                            if "kvbd" in dbg_aps:
                                for p in range(NPAIR):
                                    xqA.wait_mark(sync, f"kvbd{p}")
                                osem.inc(sync.dma_start(dbg_aps["kvbd"], kv_bd[:]), 16)

                with ExitStack() as _esCD:
                    atile = _esCD.enter_context(
                        nc.sbuf_tensor("atile", [128, NPAIR * S], BF16))
                    with ExitStack() as _es4:
                        onesb = _es4.enter_context(nc.sbuf_tensor("onesb", [128, 2048], BF16))
                        w_inq = _es4.enter_context(nc.sbuf_tensor("w_inq", [128, ND * DM], BF16))
                        qp = _es4.enter_context(nc.sbuf_tensor("qp", [128, S], BF16))
                        _fC = {k_: _es4.enter_context(
                            nc.sbuf_tensor(f"fC_{k_}", [128, S], BF16))
                            for k_ in ("a", "b", "c")}
                        fC_sc = {"sqa": _fC["a"], "pc": _fC["a"],
                                 "sqb": _fC["b"], "pb": _fC["b"],
                                 "ub": _fC["c"], "pa": _fC["c"]}
                        tqb = _es4.enter_context(nc.sbuf_tensor("tqb", [128, 20 * S], BF16))
                        ps_q = _es4.enter_context(nc.psum_tensor("ps_q", [128, 2048], F32))
                        ps_o = _es4.enter_context(nc.psum_tensor("ps_o", [128, 2048], F32))
                        blk = _es4.enter_context(nc.Block())
                        snap_fence()
                        gC0 = gsem.n
                        gmk_wq = {d: gC0 + 16 * (d + 1) for d in range(ND)}
                        tC0 = tsem.n
                        tplanC = {}
                        tc = tC0
                        tc += 1
                        tplanC["qp0"] = tc
                        tc += 1
                        tplanC["qp1"] = tc
                        for p in range(NPAIR):
                            if p + 2 < NPAIR:
                                tc += 1
                                tplanC[f"qp{p + 2}"] = tc
                            for m in range(NM):
                                tc += 1
                                tplanC[f"e2_{p}_{m}"] = tc

                        xqC = XQ(vsem, ssem)
                        xqC.v(lambda e: e.memset(onesb[:], 1.0))

                        def emit_aev(p):
                            ao = p * S
                            xqC.s((lambda p_, a_: lambda e: e.copy(
                                atile[:, a_:a_ + S], ps_o[:, 0:2048]))(p, ao),
                                  waits=[(tsem, tplanC[f"e2_{p}_{NM - 1}"])])
                            xqC.mark(f"aev{p}")
                            if "attn" in dbg_aps:
                                xqC.s((lambda p_, a_: lambda e: (st_ast.xfer(
                                    e, attn_h[p_ * 128:(p_ + 1) * 128, :],
                                    atile[:, a_:a_ + S]), e.nop())[1])(p, ao),
                                    drain=True)

                        for p in range(NPAIR):
                            xqC.v((lambda p_: lambda e: e._custom_dve(
                                CLIP, out=qp[:], in0=ps_q[:, 0:2048], s0=0.125,
                                s1=1.0, imm2=0.0))(p),
                                waits=[(tsem, tplanC[f"qp{p}"])], drain=True)
                            xqC.mark(f"qev{p}")
                            sc = {k_: t_[:] for k_, t_ in fC_sc.items()}

                            def tw(m, p_=p):
                                if p_ >= 2:
                                    return [(tsem, tplanC[f"e2_{p_ - 2}_{m}"])]
                                return []

                            emit_cheb(xqC, qp[:], sc,
                                      (lambda p_: lambda m: tqb[
                                          :, ((p_ % 2) * 10 + m - 1) * S:
                                          ((p_ % 2) * 10 + m) * S])(p),
                                      tw=tw,
                                      on_feat=(lambda m, p_=p: xqC.mark(f"tq{p_}_{m}")))
                            if p >= 1:
                                emit_aev(p - 1)
                        emit_aev(NPAIR - 1)

                        @blk.scalar
                        def _(scalar):
                            fence(scalar)
                            xqC.run("s", scalar)

                        @blk.gpsimd
                        def _(gpsimd):
                            fence(gpsimd)
                            for d in range(ND):
                                gsem.inc(gpsimd.dma_start(
                                    w_inq[:, d * DM:(d + 1) * DM],
                                    w_inT[d * 128:(d + 1) * 128, 0:DM]), 16)
                            assert gsem.n == gmk_wq[ND - 1]

                        @blk.vector
                        def _(vector):
                            fence(vector)
                            xqC.run("v", vector)

                        def emit_proj(tensor, p):
                            if p > 0:
                                xqC.wait_mark(tensor, f"qev{p - 1}")
                            for d in range(ND):
                                for sg in range(4):
                                    mm = tensor.matmul(
                                        ps_q[:, sg * 512:(sg + 1) * 512],
                                        w_inq[:, d * DM + p * 128:
                                              d * DM + (p + 1) * 128],
                                        xnT[:, d * S + sg * 512:d * S + (sg + 1) * 512],
                                        start=(d == 0), stop=(d == ND - 1))
                            tsem.inc(mm)
                            assert tsem.n == tplanC[f"qp{p}"]

                        @blk.tensor
                        def _(tensor):
                            fence(tensor)
                            tensor.wait_ge(gsem.h, gmk_wq[ND - 1])
                            emit_proj(tensor, 0)
                            emit_proj(tensor, 1)
                            for p in range(NPAIR):
                                if p + 2 < NPAIR:
                                    emit_proj(tensor, p + 2)
                                if p > 0:
                                    xqC.wait_mark(tensor, f"aev{p - 1}")
                                for m in range(NM):
                                    if m == 0:
                                        rhs_t, rof = onesb, 0
                                    else:
                                        xqC.wait_mark(tensor, f"tq{p}_{m}")
                                        rhs_t, rof = tqb, ((p % 2) * 10 + m - 1) * S
                                    for sg in range(4):
                                        mm = tensor.matmul(
                                            ps_o[:, sg * 512:(sg + 1) * 512],
                                            kv_bd[:, p * 1408 + m * 128:
                                                  p * 1408 + (m + 1) * 128],
                                            rhs_t[:, rof + sg * 512:rof + (sg + 1) * 512],
                                            start=(m == 0), stop=(m == NM - 1))
                                    tsem.inc(mm)
                                    assert tsem.n == tplanC[f"e2_{p}_{m}"]

                        if "attn" in dbg_aps:
                            @blk.sync
                            def _(sync):
                                for sm in st_ast.sems:
                                    sync.wait_ge(sm.h, sm.n)
                                osem.inc(sync.dma_start(dbg_aps["attn"], attn_h), 16)

                    if upto >= 3:
                        with ExitStack() as _es5:
                            w_out_s = _es5.enter_context(nc.sbuf_tensor("w_out_s", [128, ND * DM], BF16))
                            x1all = _es5.enter_context(nc.sbuf_tensor("x1all", [128, NSUB * DM], F32))
                            xint = _es5.enter_context(nc.sbuf_tensor("xint", [128, NSUB * DM], BF16))
                            rint = xint[:, 0:2 * DM]
                            xqt = xint[:, 2 * DM:4 * DM]
                            ps_w = _es5.enter_context(nc.psum_tensor("ps_w", [128, 2048], F32))
                            blk = _es5.enter_context(nc.Block())
                            snap_fence()
                            gD0 = gsem.n
                            gmk_x = xisem.n + 16
                            gmk_wo = {d: gD0 + 16 * (d + 1) for d in range(ND)}
                            xD0 = xsem.n
                            xq0_ = st_xq.i
                            tD0 = tsem.n
                            tplanD = {f"wo{i}": tD0 + i + 1 for i in range(NSUB)}

                            xqD = XQ(vsem, ssem)
                            for i in range(NSUB):
                                for ig in range(2):
                                    xqD.s((lambda i_, g_: lambda e: e.activation(
                                        x1all[:, i_ * DM + g_ * 512:i_ * DM + (g_ + 1) * 512],
                                        ps_w[:, (i_ % 2) * 1024 + g_ * 512:
                                             (i_ % 2) * 1024 + (g_ + 1) * 512],
                                        ACT.Copy,
                                        accum_out=st2[:, (64 if g_ else 0) + i_:
                                                      (64 if g_ else 0) + i_ + 1]))(i, ig),
                                        waits=[(tsem, tplanD[f"wo{i}"])])
                                xqD.mark(f"x1e{i}")
                                xqD.s((lambda i_: lambda e: dma_nop(
                                    e, xsem, x1_h[i_ * 128:(i_ + 1) * 128, :],
                                    x1all[:, i_ * DM:(i_ + 1) * DM]))(i), drain=True)
                                xqD.v((lambda i_: lambda e: e.reduce_max(
                                    st2[:, 80 + i_:81 + i_],
                                    x1all[:, i_ * DM:(i_ + 1) * DM],
                                    axis=AXX))(i))
                                xqD.v((lambda i_: lambda e: e.tensor_reduce(
                                    st2[:, 96 + i_:97 + i_],
                                    x1all[:, i_ * DM:(i_ + 1) * DM],
                                    AXX, ALU.min))(i))
                                xqD.s((lambda i_: lambda e: e.activation(
                                    rint[:, 0:DM],
                                    x1all[:, i_ * DM:(i_ + 1) * DM], ACT.Square,
                                    accum_out=st2[:, 16 + i_:17 + i_]))(i),
                                    drain=True)
                            xqD.v(lambda e: e.tensor_tensor(st2[:, 0:16], st2[:, 0:16],
                                                            st2[:, 64:80], ALU.add),
                                  drain=True)
                            xqD.v(lambda e: e.tensor_scalar(st2[:, 32:48], st2[:, 0:16],
                                                            1.0 / DM, None, ALU.mult), drain=True)
                            xqD.v(lambda e: e.tensor_tensor(st2[:, 48:64], st2[:, 32:48],
                                                            st2[:, 32:48], ALU.mult), drain=True)
                            xqD.v(lambda e: e.scalar_tensor_tensor(st2[:, 48:64], st2[:, 16:32],
                                                                   1.0 / DM, st2[:, 48:64],
                                                                   ALU.mult, ALU.subtract), drain=True)
                            xqD.s(lambda e: e.activation(st2[:, 48:64], st2[:, 48:64], ACT.Sqrt,
                                                         bias=wmean[:, 7:8]), drain=True)
                            xqD.v(lambda e: e.reciprocal(st2[:, 64:80], st2[:, 48:64]), drain=True)
                            xqD.v(lambda e: e.tensor_tensor(st2[:, 80:96], st2[:, 80:96],
                                                            st2[:, 32:48], ALU.subtract), drain=True)
                            xqD.v(lambda e: e.tensor_tensor(st2[:, 96:112], st2[:, 32:48],
                                                            st2[:, 96:112], ALU.subtract))
                            xqD.v(lambda e: e.tensor_tensor(st2[:, 112:128], st2[:, 80:96],
                                                            st2[:, 96:112], ALU.max), drain=True)
                            xqD.v(lambda e: e.tensor_tensor(st2[:, 112:128], st2[:, 112:128],
                                                            st2[:, 64:80], ALU.mult), drain=True)
                            xqD.v(lambda e: e.tensor_scalar(st2[:, 112:128], st2[:, 112:128],
                                                            EPS, None, ALU.max), drain=True)
                            xqD.v(lambda e: e.reciprocal(st3[:, 0:16], st2[:, 112:128]), drain=True)
                            xqD.v(lambda e: e.tensor_scalar(st3[:, 0:16], st3[:, 0:16], 127.0,
                                                            None, ALU.mult), drain=True)
                            xqD.v(lambda e: e.tensor_scalar(st3[:, 16:32], st2[:, 112:128],
                                                            1.0 / 127.0, None, ALU.mult))
                            xqD.v(lambda e: e.tensor_tensor(st2[:, 128:144], st2[:, 64:80],
                                                            st3[:, 0:16], ALU.mult), drain=True)
                            xqD.v(lambda e: e.scalar_tensor_tensor(st2[:, 144:160], st2[:, 32:48],
                                                                   -1.0, st2[:, 128:144],
                                                                   ALU.mult, ALU.mult), drain=True)
                            xqD.mark("ln2")
                            for i in range(NSUB):
                                xqD.v((lambda i_: lambda e: e._custom_dve(
                                    ROUND_AFFINE, out=rint[:, (i_ % 2) * DM:(i_ % 2) * DM + DM],
                                    in0=x1all[:, i_ * DM:(i_ + 1) * DM],
                                    s0=st2[:, 128 + i_:129 + i_], s1=st2[:, 144 + i_:145 + i_],
                                    imm2=MAGIC))(i), drain=(i == 0))
                                xqD.v((lambda i_: lambda e: e.tensor_scalar(
                                    xqt[:, (i_ % 2) * DM:(i_ % 2) * DM + DM],
                                    rint[:, (i_ % 2) * DM:(i_ % 2) * DM + DM],
                                    st3[:, 16 + i_:17 + i_], None, ALU.mult))(i), drain=True,
                                    waits=([st_xq.spec(xq0_ + i - 2)] if i >= 2 else []))
                                xqD.mark(f"xq{i}")
                                xqD.s((lambda i_: lambda e: (st_xq.xfer(
                                    e, xq_h[i_ * 128:(i_ + 1) * 128, :],
                                    xqt[:, (i_ % 2) * DM:(i_ % 2) * DM + DM]), e.nop())[1])(i))

                            @blk.gpsimd
                            def _(gpsimd):
                                fence(gpsimd)
                                xisem.inc(gpsimd.dma_start(
                                    xint[:], bass.AP(x_hd, 0, [[DM, 128],
                                                               [128 * DM, NSUB], [1, DM]])),
                                    16)
                                assert xisem.n == gmk_x
                                for d in range(ND):
                                    gsem.inc(gpsimd.dma_start(
                                        w_out_s[:, d * DM:(d + 1) * DM],
                                        w_outT_h[d * 128:(d + 1) * 128, :]), 16)

                            @blk.vector
                            def _(vector):
                                fence(vector)
                                xqD.run("v", vector)

                            @blk.scalar
                            def _(scalar):
                                fence(scalar)
                                xqD.run("s", scalar)

                            @blk.tensor
                            def _(tensor):
                                fence(tensor)
                                tensor.wait_ge(gsem.h, gmk_wo[ND - 1])
                                for i in range(NSUB):
                                    if i >= 2:
                                        xqD.wait_mark(tensor, f"x1e{i - 2}")
                                    tensor.wait_ge(xisem.h, gmk_x)
                                    for d in range(ND):
                                        for ig in range(2):
                                            tensor.matmul(
                                                ps_w[:, (i % 2) * 1024 + ig * 512:
                                                     (i % 2) * 1024 + (ig + 1) * 512],
                                                atile[:, d * S + i * 128:
                                                      d * S + (i + 1) * 128],
                                                w_out_s[:, d * DM + ig * 512:
                                                        d * DM + (ig + 1) * 512],
                                                start=(d == 0), stop=False)
                                    for ig in range(2):
                                        mm = tensor.matmul(
                                            ps_w[:, (i % 2) * 1024 + ig * 512:
                                                 (i % 2) * 1024 + (ig + 1) * 512],
                                            identb[:],
                                            xint[:, i * DM + ig * 512:i * DM + (ig + 1) * 512],
                                            start=False, stop=True)
                                    tsem.inc(mm)
                                    assert tsem.n == tplanD[f"wo{i}"]

                            if "x1" in dbg_aps or "xq" in dbg_aps:
                                @blk.sync
                                def _(sync):
                                    sync.wait_ge(xsem.h, xsem.n)
                                    for sm in st_xq.sems:
                                        sync.wait_ge(sm.h, sm.n)
                                    if "x1" in dbg_aps:
                                        osem.inc(sync.dma_start(dbg_aps["x1"], x1_h), 16)
                                    if "xq" in dbg_aps:
                                        osem.inc(sync.dma_start(dbg_aps["xq"], xq_h), 16)

        if upto >= 4:
            with ExitStack() as _es6:
                h_sb = _es6.enter_context(nc.sbuf_tensor("h_sb", [128, NCF * S], BF16))
                rmax = _es6.enter_context(nc.sbuf_tensor("rmax", [128, S], BF16))
                rmin = _es6.enter_context(nc.sbuf_tensor("rmin", [128, S], BF16))
                with ExitStack() as _es7:
                    xqT = _es7.enter_context(nc.sbuf_tensor("xqT", [128, ND * S], BF16))
                    w1f = _es7.enter_context(nc.sbuf_tensor("w1f", [128, 2 * DM], F32))
                    w1q = _es7.enter_context(nc.sbuf_tensor("w1q", [128, 2 * DM], BF16))
                    cva = _es7.enter_context(nc.sbuf_tensor("cva", [128, S], BF16))
                    cta = _es7.enter_context(nc.sbuf_tensor("cta", [128, S], BF16))
                    ctb = _es7.enter_context(nc.sbuf_tensor("ctb", [128, S], BF16))
                    ps_f = _es7.enter_context(nc.psum_tensor("ps_f", [128, 4096], F32))
                    blk = _es7.enter_context(nc.Block())
                    snap_fence()
                    xF0 = xsem.n
                    xmk_xqT = {d: xF0 + 16 * (d + 1) for d in range(ND)}
                    w10 = st_w1.i
                    tF0 = tsem.n
                    tplanF = {f"f{c}": tF0 + c + 1 for c in range(NCF)}

                    xqF = XQ(vsem, ssem)

                    def emit_w1q(c):
                        xqF.v((lambda c_: lambda e: e._custom_dve(
                            WQUANT, out=w1q[:, (c_ % 2) * DM:(c_ % 2) * DM + DM],
                            in0=w1f[:, (c_ % 2) * DM:(c_ % 2) * DM + DM],
                            s0=wmean[:, 0:1], s1=1.0, imm2=MAGIC))(c),
                            waits=[st_w1.spec(w10 + c)] +
                            ([(tsem, tplanF[f"f{c - 2}"])] if c >= 2 else []))
                        xqF.mark(f"w1q{c}")

                    def emit_rm(c, half=None, drain=False):
                        h_ = h_sb[:, c * S:(c + 1) * S]
                        halves = (0, 1) if half is None else (half,)
                        for hh in halves:
                            buf = rmax if hh == 0 else rmin
                            op = ALU.max if hh == 0 else ALU.min
                            if c == 0:
                                xqF.v((lambda h2, b_: lambda e: e.tensor_copy(
                                    b_[:], h2))(h_, buf), drain=drain)
                            else:
                                xqF.v((lambda h2, b_, o_: lambda e: e.tensor_tensor(
                                    b_[:], b_[:], h2, o_))(h_, buf, op),
                                    drain=drain)
                        if half in (None, 1):
                            xqF.mark(f"rm{c}")

                    emit_w1q(0)
                    for c in range(NCF):
                        if c >= 1:
                            emit_rm(c - 1, half=0)
                        if c + 1 < NCF:
                            emit_w1q(c + 1)
                        ho = c * S
                        h_c = h_sb[:, ho:ho + S]
                        pr = (c % 2) * 2048
                        for sg in range(4):
                            xqF.s((lambda c_, g_, r_: lambda e: e.copy(
                                h_sb[:, c_ * S + g_ * 512:c_ * S + (g_ + 1) * 512],
                                ps_f[:, r_ + g_ * 512:r_ + (g_ + 1) * 512]))(c, sg, pr),
                                waits=[(tsem, tplanF[f"f{c}"])])
                        xqF.mark(f"hev{c}")
                        cw0 = convw[:, c * 3:c * 3 + 1]
                        cw1 = convw[:, c * 3 + 1:c * 3 + 2]
                        cw2 = convw[:, c * 3 + 2:c * 3 + 3]
                        cb = convb[:, c:c + 1]
                        xqF.v((lambda h_, w_, b_: lambda e: e.tensor_scalar(
                            cva[:], h_, w_, b_, ALU.mult, ALU.add))(h_c, cw1, cb),
                            drain=True)
                        xqF.v((lambda h_, w_: lambda e: e.tensor_scalar(
                            cta[:], h_, w_, None, ALU.mult))(h_c, cw0))
                        xqF.v((lambda h_, w_: lambda e: e.tensor_scalar(
                            ctb[:], h_, w_, None, ALU.mult))(h_c, cw2))
                        if c >= 1:
                            emit_rm(c - 1, half=1, drain=True)
                        xqF.v(lambda e: e.tensor_tensor(
                            cva[:, 1:S], cva[:, 1:S], cta[:, 0:S - 1], ALU.add),
                            drain=(c == 0))
                        xqF.v(lambda e: e.tensor_tensor(
                            cva[:, 0:S - 1], cva[:, 0:S - 1], ctb[:, 1:S],
                            ALU.add), drain=True)
                        xqF.s((lambda h_: lambda e: e.activation(
                            h_, cva[:], ACT.Silu))(h_c), drain=True)
                        xqF.mark(f"silu{c}")
                    emit_rm(NCF - 1)

                    @blk.scalar
                    def _(scalar):
                        fence(scalar)
                        for d in range(ND):
                            xsem.inc(scalar.dma_start(
                                xqT[:, d * S:(d + 1) * S],
                                xq_h[:, d * 128:(d + 1) * 128], transpose=True), 16)
                        assert xsem.n == xmk_xqT[ND - 1]
                        xqF.run("s", scalar)

                    @blk.sync
                    def _(sync):
                        fence(sync)
                        for c in range(NCF):
                            if c >= 2:
                                xqF.wait_mark(sync, f"w1q{c - 2}")
                            st_w1.xfer(sync,
                                       w1f[:, (c % 2) * DM:(c % 2) * DM + DM],
                                       w1T_h[c * 128:(c + 1) * 128, :])

                    @blk.vector
                    def _(vector):
                        fence(vector)
                        xqF.run("v", vector)

                    if "xqt" in dbg_aps:
                        @blk.sync
                        def _(sync):
                            sync.wait_ge(xsem.h, xmk_xqT[ND - 1])
                            osem.inc(sync.dma_start(dbg_aps["xqt"], xqT[:]), 16)

                    if "hs" in dbg_aps or "rmx" in dbg_aps:
                        @blk.sync
                        def _(sync):
                            for c in range(NCF):
                                xqF.wait_mark(sync, f"rm{c}")
                            if "hs" in dbg_aps:
                                osem.inc(sync.dma_start(dbg_aps["hs"], h_sb[:]), 16)
                            if "rmx" in dbg_aps:
                                osem.inc(sync.dma_start(dbg_aps["rmx"][0:128, 0:S],
                                                        rmax[:]), 16)
                                osem.inc(sync.dma_start(dbg_aps["rmx"][0:128, S:2 * S],
                                                        rmin[:]), 16)

                    @blk.tensor
                    def _(tensor):
                        fence(tensor)
                        tensor.wait_ge(xsem.h, xmk_xqT[ND - 1])
                        for c in range(NCF):
                            xqF.wait_mark(tensor, f"w1q{c}")
                            if c >= 2:
                                xqF.wait_mark(tensor, f"hev{c - 2}")
                            pr = (c % 2) * 2048
                            for d in range(ND):
                                for sg in range(4):
                                    mm = tensor.matmul(
                                        ps_f[:, pr + sg * 512:pr + (sg + 1) * 512],
                                        w1q[:, (c % 2) * DM + d * 128:
                                            (c % 2) * DM + (d + 1) * 128],
                                        xqT[:, d * S + sg * 512:d * S + (sg + 1) * 512],
                                        start=(d == 0), stop=(d == ND - 1))
                            tsem.inc(mm)
                            assert tsem.n == tplanF[f"f{c}"]

                with ExitStack() as _es8:
                    selb = _es8.enter_context(nc.sbuf_tensor("selb", [16, NSUB * 128], F32))
                    stile = _es8.enter_context(nc.sbuf_tensor("stile", [128, S], F32))
                    comb = _es8.enter_context(nc.sbuf_tensor("comb", [128, S], BF16))
                    sp16 = _es8.enter_context(nc.sbuf_tensor("sp16", [16, 128], F32))
                    ps_t = _es8.enter_context(nc.psum_tensor("ps_t", [128, 2048], BF16))
                    ps_s = _es8.enter_context(nc.psum_tensor("ps_s", [128, 2048], F32))
                    ps_sp = _es8.enter_context(nc.psum_tensor("ps_sp", [16, 128], F32))
                    blk = _es8.enter_context(nc.Block())
                    snap_fence()
                    dFq0 = dsem.n
                    dmk_sel = dFq0 + 16
                    xFq0 = xsem.n
                    xmk_hq = {c: xFq0 + 16 * (c + 1) for c in range(NCF)}
                    tFq0 = tsem.n
                    tplanQ = {"trs": tFq0 + 1, "spt": tFq0 + 2, "bc": tFq0 + 3}

                    xqQ = XQ(vsem, ssem)
                    xqQ.v(lambda e: e.scalar_tensor_tensor(comb[:], rmin[:], -1.0,
                                                           rmax[:], ALU.mult, ALU.max),
                          drain=True)
                    xqQ.mark("comb")
                    for j in range(NSUB):
                        xqQ.v((lambda j_: lambda e: e.reduce_max(
                            st2[:, j_:j_ + 1],
                            ps_t[:, j_ * 128:(j_ + 1) * 128], axis=AXX))(j),
                            waits=[(tsem, tplanQ["trs"])])
                    xqQ.mark("amax2")
                    xqQ.v(lambda e: e.tensor_scalar(st2[:, 0:16], st2[:, 0:16], EPS,
                                                    None, ALU.max), drain=True)
                    xqQ.v(lambda e: e.reciprocal(st2[:, 16:32], st2[:, 0:16]),
                          drain=True)
                    xqQ.v(lambda e: e.tensor_scalar(st2[:, 16:32], st2[:, 16:32],
                                                    127.0, None, ALU.mult), drain=True)
                    xqQ.v(lambda e: e.tensor_scalar(st2[:, 32:48], st2[:, 0:16],
                                                    1.0 / 127.0, wmean[:, 3:4],
                                                    ALU.mult, ALU.mult))
                    xqQ.mark("s2tok")
                    xqQ.s(lambda e: e.copy(sp16[:], ps_sp[:]),
                          waits=[(tsem, tplanQ["spt"])])
                    xqQ.mark("sp")
                    for g in range(4):
                        xqQ.v((lambda g_: lambda e: e.tensor_copy(
                            stile[:, g_ * 512:(g_ + 1) * 512],
                            ps_s[:, g_ * 512:(g_ + 1) * 512]))(g),
                            waits=[(tsem, tplanQ["bc"])])
                    xqQ.mark("stile")
                    for c in range(NCF):
                        h_c = h_sb[:, c * S:(c + 1) * S]
                        xqQ.v((lambda h_: lambda e: e._custom_dve(
                            ROUND_MUL, out=h_, in0=h_, in1=stile[:], s0=0.0,
                            imm2=MAGIC))(h_c), drain=True)
                        xqQ.mark(f"hq{c}")

                    @blk.sync
                    def _(sync):
                        fence(sync)
                        dsem.inc(sync.dma_start(selb[:], sel_in), 16)
                        assert dsem.n == dmk_sel
                        for c in range(NCF):
                            xqQ.wait_mark(sync, f"hq{c}")
                            xsem.inc(sync.dma_start(
                                hq_h[c * 128:(c + 1) * 128, :],
                                h_sb[:, c * S:(c + 1) * S]), 16)

                    @blk.vector
                    def _(vector):
                        fence(vector)
                        xqQ.run("v", vector)

                    @blk.scalar
                    def _(scalar):
                        fence(scalar)
                        xqQ.run("s", scalar)

                    @blk.tensor
                    def _(tensor):
                        fence(tensor)
                        xqQ.wait_mark(tensor, "comb")
                        for j in range(NSUB):
                            tr = tensor.matmul(
                                ps_t[:, j * 128:(j + 1) * 128],
                                comb[:, j * 128:(j + 1) * 128], identb[:],
                                start=(j in (0, 8)), stop=(j in (7, 15)),
                                is_transpose=True)
                        tsem.inc(tr)
                        assert tsem.n == tplanQ["trs"]
                        xqQ.wait_mark(tensor, "s2tok")
                        tsem.inc(tensor.matmul(ps_sp[:], st2[:, 16:32], identf[:],
                                               start=True, stop=True,
                                               is_transpose=True))
                        assert tsem.n == tplanQ["spt"]
                        xqQ.wait_mark(tensor, "sp")
                        tensor.wait_ge(dsem.h, dmk_sel)
                        for j in range(NSUB):
                            mm = tensor.matmul(ps_s[:, j * 128:(j + 1) * 128],
                                               selb[0:16, j * 128:(j + 1) * 128],
                                               sp16[0:16, 0:128],
                                               start=(j % 4 == 0),
                                               stop=(j % 4 == 3))
                        tsem.inc(mm)
                        assert tsem.n == tplanQ["bc"]

                    if "hq" in dbg_aps or "stile" in dbg_aps:
                        @blk.sync
                        def _(sync):
                            sync.wait_ge(xsem.h, xsem.n)
                            if "hq" in dbg_aps:
                                osem.inc(sync.dma_start(dbg_aps["hq"], hq_h), 16)
                            if "stile" in dbg_aps:
                                xqQ.wait_mark(sync, "stile")
                                osem.inc(sync.dma_start(dbg_aps["stile"], stile[:]), 16)

        if upto >= 6:
            with ExitStack() as _es9:
                hqs = _es9.enter_context(nc.sbuf_tensor("hqs", [128, 4 * DFF], BF16))
                w2f = _es9.enter_context(nc.sbuf_tensor("w2f", [128, 16 * DM], F32))
                w2q = _es9.enter_context(nc.sbuf_tensor("w2q", [128, NCF * DM], BF16))
                x1t = _es9.enter_context(nc.sbuf_tensor("x1t", [128, 2 * DM], F32))
                outt = _es9.enter_context(nc.sbuf_tensor("outt", [128, 2 * DM], F32))
                ps_g = _es9.enter_context(nc.psum_tensor("ps_g", [128, 2048], F32))
                blk = _es9.enter_context(nc.Block())
                snap_fence()
                hq0 = st_hq.i
                x1t0 = st_x1t.i
                w20 = st_w2.i
                out0 = st_out.i
                tG0 = tsem.n
                tplanG = {f"g{i}": tG0 + i + 1 for i in range(NSUB)}

                xqG = XQ(vsem, ssem)
                for c in range(NCF):
                    xqG.v((lambda c_: lambda e: e._custom_dve(
                        WQUANT, out=w2q[:, c_ * DM:(c_ + 1) * DM],
                        in0=w2f[:, (c_ % 16) * DM:(c_ % 16) * DM + DM],
                        s0=wmean[:, 2:3], s1=1.0, imm2=MAGIC))(c),
                        waits=[st_w2.spec(w20 + c)], drain=True)
                    xqG.mark(f"w2q{c}")
                for i in range(NSUB):
                    for ig in range(2):
                        xqG.v((lambda i_, g_: lambda e: e.scalar_tensor_tensor(
                            outt[:, (i_ % 2) * DM + g_ * 512:
                                 (i_ % 2) * DM + (g_ + 1) * 512],
                            ps_g[:, (i_ % 2) * 1024 + g_ * 512:
                                 (i_ % 2) * 1024 + (g_ + 1) * 512],
                            st2[:, 32 + i_:33 + i_],
                            x1t[:, (i_ % 2) * DM + g_ * 512:
                                (i_ % 2) * DM + (g_ + 1) * 512],
                            ALU.mult, ALU.add))(i, ig),
                            waits=[(tsem, tplanG[f"g{i}"]),
                                   st_x1t.spec(x1t0 + i)] +
                            ([st_out.spec(out0 + i - 2)] if i >= 2 else []))
                    xqG.mark(f"oev{i}")
                    xqG.s((lambda i_: lambda e: (st_out.xfer(
                        e, out_hb[i_ * 128:(i_ + 1) * 128, :],
                        outt[:, (i_ % 2) * DM:(i_ % 2) * DM + DM]),
                        e.nop())[1])(i), drain=True)
                    xqG.mark(f"ost{i}")

                @blk.sync
                def _(sync):
                    fence(sync)
                    for i in range(NSUB):
                        if i >= 4:
                            sync.wait_ge(tsem.h, tplanG[f"g{i - 4}"])
                        if i >= 2:
                            sync.wait_ge(vsem.h, xqG.marks[f"oev{i - 2}"][0])
                        srcap = bass.AP(hq_hd, i * 128,
                                        [[S, 128], [128 * S, NCF], [1, 128]])
                        st_hq.xfer(sync, hqs[:, (i % 4) * DFF:
                                             (i % 4) * DFF + DFF], srcap)
                        st_x1t.xfer(sync,
                                    x1t[:, (i % 2) * DM:(i % 2) * DM + DM],
                                    x1_h[i * 128:(i + 1) * 128, :])
                    for sm in st_out.sems:
                        sync.wait_ge(sm.h, sm.n)

                @blk.vector
                def _(vector):
                    fence(vector)
                    xqG.run("v", vector)

                @blk.scalar
                def _(scalar):
                    fence(scalar)
                    for ci in range(NCF):
                        if ci >= 16:
                            scalar.wait_ge(vsem.h,
                                           xqG.marks[f"w2q{ci - 16}"][0])
                        st_w2.xfer(scalar,
                                   w2f[:, (ci % 16) * DM:(ci % 16) * DM + DM],
                                   w2T_h[ci * 128:(ci + 1) * 128, :])
                    xqG.run("s", scalar)
                    for sm in st_out.sems:
                        scalar.wait_ge(sm.h, sm.n)

                @blk.tensor
                def _(tensor):
                    fence(tensor)
                    for i in range(NSUB):
                        st_hq.wait_idx(tensor, hq0 + i)
                        if i >= 2:
                            xqG.wait_mark(tensor, f"oev{i - 2}")
                        for c in range(NCF):
                            xqG.wait_mark(tensor, f"w2q{c}")
                            for ig in range(2):
                                mm = tensor.matmul(
                                    ps_g[:, (i % 2) * 1024 + ig * 512:
                                         (i % 2) * 1024 + (ig + 1) * 512],
                                    hqs[:, (i % 4) * DFF + c * 128:
                                        (i % 4) * DFF + (c + 1) * 128],
                                    w2q[:, c * DM + ig * 512:c * DM + (ig + 1) * 512],
                                    start=(c == 0), stop=(c == NCF - 1))
                        tsem.inc(mm)
                        assert tsem.n == tplanG[f"g{i}"]

    mybir.codegen_inst_isa_subclasses(nc)
    return nc, dbg_aps


def _host_inputs(core_x, L0_attn_w_in, L0_attn_w_out, L0_attn_beta,
                 L0_ffn_w1, L0_ffn_conv_w, L0_ffn_conv_b, L0_ffn_w2):
    f32c = lambda a: np.ascontiguousarray(np.asarray(a, dtype=np.float32))
    w_inT = f32c(np.asarray(L0_attn_w_in).T)
    w_outT = f32c(np.asarray(L0_attn_w_out).T)
    w1T = f32c(np.asarray(L0_ffn_w1, dtype=np.float32)
               .reshape(NCF, 128, ND, 128).transpose(0, 3, 2, 1)
               .reshape(DFF, DM))
    w2T = f32c(np.asarray(L0_ffn_w2).T)
    convw = f32c(np.asarray(L0_ffn_conv_w).reshape(NCF, 128, 3)
                 .transpose(1, 0, 2).reshape(128, NCF * 3))
    convb = f32c(np.asarray(L0_ffn_conv_b).reshape(NCF, 128).T)
    beta_t = f32c(np.tile(np.asarray(L0_attn_beta).reshape(1, 176), (128, 1)))
    identb = np.eye(128).astype(bf16)
    identf = np.eye(128, dtype=np.float32)
    onesf = np.ones((128, 128), np.float32)
    sel = np.zeros((16, NSUB * 128), np.float32)
    for j in range(16):
        sel[j, j * 128:(j + 1) * 128] = 1.0
    base = {"w_inT": w_inT, "w_outT": w_outT, "w1T": w1T, "w2T": w2T,
            "convw": convw, "convb": convb, "beta_t": beta_t,
            "identb": identb, "identf": identf, "onesf": onesf, "sel": sel}
    maps = []
    for b in range(core_x.shape[0]):
        m = dict(base)
        m["x"] = np.ascontiguousarray(np.asarray(core_x[b], dtype=np.float32))
        maps.append(m)
    return maps


_BUILD_CACHE = {}


def kernel(L0_attn_x, L0_attn_w_in, L0_attn_w_out, L0_attn_beta,
           L0_ffn_w1, L0_ffn_conv_w, L0_ffn_conv_b, L0_ffn_w2):
    x = np.asarray(L0_attn_x, dtype=np.float32)
    nb = x.shape[0]
    if "nc" not in _BUILD_CACHE:
        _BUILD_CACHE["nc"] = build()[0]
    nc = _BUILD_CACHE["nc"]
    in_maps = _host_inputs(x, L0_attn_w_in, L0_attn_w_out, L0_attn_beta,
                           L0_ffn_w1, L0_ffn_conv_w, L0_ffn_conv_b, L0_ffn_w2)
    res = run_bass_kernel_spmd(nc, in_maps, core_ids=list(range(nb)))
    out = np.stack([np.asarray(r["out"], dtype=np.float32)
                    for r in res.results], axis=0)
    return out
